# version 04
# baseline (speedup 1.0000x reference)
import time
import numpy as np
import ml_dtypes
from contextlib import ExitStack

import concourse.bass as bass
import concourse.mybir as mybir
from concourse.bass_utils import run_bass_kernel_spmd

F32 = mybir.dt.float32
BF16 = mybir.dt.bfloat16
I32 = mybir.dt.int32
U8 = mybir.dt.uint8
AF = mybir.ActivationFunctionType
ALU = mybir.AluOpType
AXF = mybir.AxisListType.X
bfdt = ml_dtypes.bfloat16
ds = bass.ds

B, SEQ, D = 4, 4096, 2048
T = 2048
NTT = T // 128
NDC = D // 128
NH, HP, NKV, DXB, DCONV, CHUNK = 16, 128, 4, 512, 4, 256
NCH = T // CHUNK
CONVDIM = 3072
FIN = 5136
W1 = T + 3
MLP = 16384
AH, AE = 16, 128

N_PHASES = 9
OUT_SRC = "x_res"
MAMBA_STEPS = 9
SSD_POLL = True
POLL_MODE = 2
SSD_HEADS = NH
DBG_NAME = None

FLAG_STATE0 = 0
FLAG_KV = 16
FLAG_DQ = 18
NFLAGS = 24

_prog_cache = {}



def _prep_static(inputs):
    s = {}
    s["WinT"] = np.ascontiguousarray(np.asarray(inputs["L0_attn_a_in_proj_w"], np.float32).T)
    cw = np.asarray(inputs["L0_attn_a_conv_w"], np.float32)
    s["convw"] = np.ascontiguousarray(cw.reshape(24, 128, 4).transpose(1, 0, 2).reshape(128, 96))
    s["convb"] = np.ascontiguousarray(np.asarray(inputs["L0_attn_a_conv_b"], np.float32).reshape(24, 128).T)
    s["dtb"] = np.asarray(inputs["L0_attn_a_dt_bias"], np.float32).reshape(16, 1).copy()
    s["alog"] = np.asarray(inputs["L0_attn_a_A_log"], np.float32).reshape(16, 1).copy()
    s["Drow"] = np.asarray(inputs["L0_attn_a_D"], np.float32).reshape(1, 16).copy()
    s["normw"] = np.ascontiguousarray(np.asarray(inputs["L0_attn_a_norm_w"], np.float32).reshape(16, 128).T)
    s["WoutT"] = np.ascontiguousarray(np.asarray(inputs["L0_attn_a_out_proj_w"], np.float32).T)
    s["wqkv"] = np.ascontiguousarray(np.asarray(inputs["L0_attn_b_w_qkv"], np.float32))
    s["bqkv"] = np.ascontiguousarray(np.asarray(inputs["L0_attn_b_b_qkv"], np.float32).reshape(48, 128).T)
    s["bqkv_vr"] = np.asarray(inputs["L0_attn_b_b_qkv"], np.float32)[4096:6144].reshape(1, 2048).copy()
    s["wout"] = np.ascontiguousarray(np.asarray(inputs["L0_attn_b_w_out"], np.float32))
    s["bout"] = np.asarray(inputs["L0_attn_b_b_out"], np.float32).reshape(1, 2048).copy()
    s["g_pT"] = np.ascontiguousarray(inputs["L0_ffn_gate_W_packed"].astype(np.uint8).T)
    s["u_pT"] = np.ascontiguousarray(inputs["L0_ffn_up_W_packed"].astype(np.uint8).T)
    s["d_pT"] = np.ascontiguousarray(inputs["L0_ffn_down_W_packed"].astype(np.uint8).T)
    s["g_scT"] = np.ascontiguousarray(np.asarray(inputs["L0_ffn_gate_scales"], np.float32).T)
    s["u_scT"] = np.ascontiguousarray(np.asarray(inputs["L0_ffn_up_scales"], np.float32).T)
    s["d_scT"] = np.ascontiguousarray(np.asarray(inputs["L0_ffn_down_scales"], np.float32).T)
    return s


def kernel(**inputs):
    if "prog" not in _prog_cache:
        _prog_cache["prog"] = build_program()
    nc = _prog_cache["prog"]

    st = _prep_static(inputs)
    X = np.asarray(inputs["L0_attn_a_hidden_states"], np.float32)
    nonce = int(time.time_ns() & 0x3FFFFFFF) | 1
    flags = (nonce + np.arange(NFLAGS, dtype=np.int64)).astype(np.int32).reshape(1, NFLAGS)

    in_maps = []
    for c in range(8):
        b, h = c // 2, c % 2
        t0 = h * T
        im = {"x_in": np.ascontiguousarray(X[b, t0:t0 + T]),
              "xh_in": (np.zeros((3, D), np.float32) if h == 0
                        else np.ascontiguousarray(X[b, t0 - 3:t0])),
              "flags_in": flags}
        im.update(st)
        in_maps.append(im)

    res = run_bass_kernel_spmd(nc, in_maps, core_ids=list(range(8)))
    out = np.empty((B, SEQ, D), np.float32)
    for c in range(8):
        b, h = c // 2, c % 2
        out[b, h * T:(h + 1) * T] = res.results[c]["y"]
    kernel.dbg = [res.results[c].get("dbg") for c in range(8)]
    return out



class Sem:
    def __init__(self, nc, name):
        self.h = nc.alloc_semaphore(name)
        self.n = 0
        self.base = 0
        self.swdge = False

    def inc(self, instr, k=1):
        instr.then_inc(self.h, k)
        self.n += k
        return instr

    def dma(self, instr):
        return self.inc(instr, 16)

    def wait(self, eng, n=None):
        eng.wait_ge(self.h, self.n if n is None else n)


class P:
    pass


def build_program():
    nc = bass.Bass("TRN2", target_bir_lowering=False, detect_race_conditions=False)
    p = P()
    p.nc = nc

    ap = lambda t: t.ap()
    p.x_in = ap(nc.dram_tensor("x_in", [T, D], F32, kind="ExternalInput"))
    p.xh_in = ap(nc.dram_tensor("xh_in", [3, D], F32, kind="ExternalInput"))
    p.flags_in = ap(nc.dram_tensor("flags_in", [1, NFLAGS], I32, kind="ExternalInput"))
    p.WinT = ap(nc.dram_tensor("WinT", [D, FIN], F32, kind="ExternalInput"))
    p.convw_d = ap(nc.dram_tensor("convw", [128, 96], F32, kind="ExternalInput"))
    p.convb_d = ap(nc.dram_tensor("convb", [128, 24], F32, kind="ExternalInput"))
    p.dtb_d = ap(nc.dram_tensor("dtb", [16, 1], F32, kind="ExternalInput"))
    p.alog_d = ap(nc.dram_tensor("alog", [16, 1], F32, kind="ExternalInput"))
    p.Drow_d = ap(nc.dram_tensor("Drow", [1, 16], F32, kind="ExternalInput"))
    p.normw_d = ap(nc.dram_tensor("normw", [128, 16], F32, kind="ExternalInput"))
    p.WoutT = ap(nc.dram_tensor("WoutT", [D, D], F32, kind="ExternalInput"))
    p.wqkv = ap(nc.dram_tensor("wqkv", [D, 3 * D], F32, kind="ExternalInput"))
    p.bqkv_d = ap(nc.dram_tensor("bqkv", [128, 48], F32, kind="ExternalInput"))
    p.bqkv_vr = ap(nc.dram_tensor("bqkv_vr", [1, D], F32, kind="ExternalInput"))
    p.wout = ap(nc.dram_tensor("wout", [D, D], F32, kind="ExternalInput"))
    p.bout_d = ap(nc.dram_tensor("bout", [1, D], F32, kind="ExternalInput"))
    p.g_pT = ap(nc.dram_tensor("g_pT", [1024, MLP], U8, kind="ExternalInput"))
    p.u_pT = ap(nc.dram_tensor("u_pT", [1024, MLP], U8, kind="ExternalInput"))
    p.d_pT = ap(nc.dram_tensor("d_pT", [MLP // 2, D], U8, kind="ExternalInput"))
    p.g_scT = ap(nc.dram_tensor("g_scT", [64, MLP], F32, kind="ExternalInput"))
    p.u_scT = ap(nc.dram_tensor("u_scT", [64, MLP], F32, kind="ExternalInput"))
    p.d_scT = ap(nc.dram_tensor("d_scT", [512, D], F32, kind="ExternalInput"))

    p.y = ap(nc.dram_tensor("y", [T, D], F32, kind="ExternalOutput"))
    p.dbg = ap(nc.dram_tensor("dbg", [128, 4096], F32, kind="ExternalOutput"))

    p.x_res = ap(nc.dram_tensor("x_res", [T, D], F32, kind="Internal"))
    p.z_sp = ap(nc.dram_tensor("z_sp", [16, 128, T], BF16, kind="Internal"))
    p.xbc_sp = ap(nc.dram_tensor("xbc_sp", [24, 128, W1], BF16, kind="Internal"))
    p.c_sp = ap(nc.dram_tensor("c_sp", [16, 128, T], BF16, kind="Internal"))
    p.acs_sp = ap(nc.dram_tensor("acs_sp", [16, T], F32, kind="Internal"))
    p.eacs_sp = ap(nc.dram_tensor("eacs_sp", [16, T], BF16, kind="Internal"))
    p.cd_sp = ap(nc.dram_tensor("cd_sp", [16, NCH], F32, kind="Internal"))
    p.rstd_sp = ap(nc.dram_tensor("rstd_sp", [16, T], F32, kind="Internal"))
    p.yn_sp = ap(nc.dram_tensor("yn_sp", [16, 128, T], BF16, kind="Internal"))
    p.ao_sp = ap(nc.dram_tensor("ao_sp", [16, 128, T], BF16, kind="Internal"))
    p.dnm_sp = ap(nc.dram_tensor("dnm_sp", [16, T], F32, kind="Internal"))
    p.int_sp = ap(nc.dram_tensor("int_sp", [128, 8192], BF16, kind="Internal"))

    p.sh_state = ap(nc.dram_tensor("sh_state", [NH, 128, 128], F32, kind="Internal", addr_space="Shared"))
    p.sh_k = ap(nc.dram_tensor("sh_k", [2, AH, 128, T], BF16, kind="Internal", addr_space="Shared"))
    p.sh_v = ap(nc.dram_tensor("sh_v", [2, AH, T, 128], BF16, kind="Internal", addr_space="Shared"))
    p.sh_wg = ap(nc.dram_tensor("sh_wg", [16, 128, MLP], BF16, kind="Internal", addr_space="Shared"))
    p.sh_wu = ap(nc.dram_tensor("sh_wu", [16, 128, MLP], BF16, kind="Internal", addr_space="Shared"))
    p.sh_wd = ap(nc.dram_tensor("sh_wd", [128, 128, D], BF16, kind="Internal", addr_space="Shared"))
    p.sh_flg = ap(nc.dram_tensor("sh_flg", [NFLAGS, 16], I32, kind="Internal", addr_space="Shared"))

    p.ident_d = ap(nc.inline_tensor(np.eye(128, dtype=bfdt), "identc"))
    p.identf_d = ap(nc.inline_tensor(np.eye(128, dtype=np.float32), "identfc"))
    tri_a = np.concatenate([np.triu(np.ones((128, 128))), np.ones((128, 128))], 1)
    tri_b = np.concatenate([np.zeros((128, 128)), np.triu(np.ones((128, 128)))], 1)
    p.triA_d = ap(nc.inline_tensor(tri_a.astype(np.float32), "triAc"))
    p.triB_d = ap(nc.inline_tensor(tri_b.astype(np.float32), "triBc"))
    p.ones16_d = ap(nc.inline_tensor(np.ones((16, 256), np.float32), "ones16c"))
    p.onescol_d = ap(nc.inline_tensor(np.ones((128, 8), dtype=bfdt), "onescolc"))
    p.zeros_d = ap(nc.inline_tensor(np.zeros((128, 128), np.float32), "zerosc"))
    cc = np.zeros((128, 8), np.float32)
    cc[:, 0] = 1e-5
    cc[:, 1] = 1.0
    cc[:, 2] = -2.0
    cc[:, 3] = -6.0
    p.cc_d = ap(nc.inline_tensor(cc, "constcols"))

    A = nc.alloc_sbuf_tensor
    p.ident = A("identsb", [128, 128], BF16).ap()
    p.identf = A("identfsb", [128, 128], F32).ap()
    p.triA = A("triAsb", [128, 256], F32).ap()
    p.triB = A("triBsb", [128, 256], F32).ap()
    p.ones16 = A("ones16sb", [16, 256], F32).ap()
    p.onescol = A("onescolsb", [128, 8], BF16).ap()
    p.convw = A("convwsb", [128, 96], F32).ap()
    p.convb = A("convbsb", [128, 24], F32).ap()
    p.dtb = A("dtbsb", [16, 1], F32).ap()
    p.alog = A("alogsb", [16, 1], F32).ap()
    p.Acol = A("Acolsb", [16, 1], F32).ap()
    p.normw = A("normwsb", [128, 16], F32).ap()
    p.Db = A("Dbsb", [128, 16], F32).ap()
    p.bqkv = A("bqkvsb", [128, 48], F32).ap()
    p.bout_b = A("boutsb", [128, D], F32).ap()
    p.fin = A("finsb", [1, NFLAGS], I32).ap()
    p.fpoll = A("fpollsb", [1, 16], I32).ap()
    p.cc = A("ccsb", [128, 8], F32).ap()
    p.lnstat = A("lnstatsb", [128, 16], F32).ap()

    p.psA = nc.alloc_psum_tensor("psA", [128, 1024], F32).ap()
    p.psB = nc.alloc_psum_tensor("psB", [128, 1024], F32).ap()
    p.psC = nc.alloc_psum_tensor("psC", [128, 2048], F32).ap()

    p._pool = {False: [], True: []}
    p._leased = []

    def newsem(name, g=False, keep=False):
        pool = p._pool[g]
        if pool:
            s = pool.pop()
        else:
            s = Sem(nc, name)
            s.swdge = g
        s.base = s.n
        if not keep:
            p._leased.append(s)
        return s
    p.newsem = newsem

    def phase_release():
        for s in p._leased:
            p._pool[s.swdge].append(s)
        p._leased = []
    p.phase_release = phase_release

    p.chA = Sem(nc, "chainA")
    p.chV = Sem(nc, "chainV")
    p.ph = Sem(nc, "phase")
    p.core = nc.partition_id()
    p.me = p.core % 2
    p.other = 1 - p.me

    with nc.Block() as block:
        p.block = block
        emit_prelude(p)
        if N_PHASES >= 1:
            emit_mamba(p)
        if N_PHASES >= 2:
            emit_dequant(p)
            emit_ffn(p, 1)
        if N_PHASES >= 3:
            emit_attn(p)
        if N_PHASES >= 4:
            emit_ffn(p, 2)
        emit_output(p)
    nc.finalize()
    return nc


def barrier(p):
    ph, block = p.ph, p.block
    dummy = p.lnstat[0:1, 15:16] if hasattr(p, "lnstat") else None

    def f_v(v):
        if dummy is not None:
            ph.inc(v.tensor_copy(out=dummy, in_=dummy))
        else:
            ph.inc(v.nop())
    block.vector(f_v)

    def f_sc(sc):
        if dummy is not None:
            ph.inc(sc.activation(out=dummy, in_=dummy, func=AF.Copy))
        else:
            ph.inc(sc.nop())
    block.scalar(f_sc)

    for attr in ("sync", "tensor", "gpsimd"):
        getattr(block, attr)(lambda eng: ph.inc(eng.nop()))
    tgt = ph.n
    for attr in ("sync", "tensor", "vector", "scalar", "gpsimd"):
        getattr(block, attr)(lambda eng, tgt=tgt: eng.wait_ge(ph.h, tgt))


def dbg_tap(p, name, src_dram, rows, cols):
    if DBG_NAME != name:
        return
    nc, block = p.nc, p.block
    s_ = p.newsem("dbgs")

    def f(eng):
        with nc.sbuf_tensor("dbgbuf", [128, 4096], F32) as bt:
            bb = bt.ap()
            for r0 in range(0, rows, 128):
                r1 = min(rows, r0 + 128)
                s_.dma(eng.dma_start(out=bb[0:r1 - r0, 0:cols], in_=src_dram[r0:r1, 0:cols]))
                s_.wait(eng)
                s_.dma(eng.dma_start(out=p.dbg[r0:r1, 0:cols], in_=bb[0:r1 - r0, 0:cols]))
                s_.wait(eng)
    block.sync(f)
    barrier(p)



def emit_prelude(p):
    block = p.block
    ld = p.newsem("pl_ld")
    ldg = p.newsem("pl_ldg", g=True)
    ev = p.newsem("pl_ev")

    def f_sync(s):
        for dst, src in [(p.ident, p.ident_d), (p.identf, p.identf_d),
                         (p.triA, p.triA_d), (p.triB, p.triB_d),
                         (p.ones16, p.ones16_d), (p.onescol, p.onescol_d),
                         (p.convw, p.convw_d), (p.convb, p.convb_d),
                         (p.dtb, p.dtb_d), (p.alog, p.alog_d), (p.normw, p.normw_d),
                         (p.bqkv, p.bqkv_d), (p.fin, p.flags_in),
                         (p.cc, p.cc_d)]:
            ld.dma(s.dma_start(out=dst[:, :], in_=src[:, :]))
        ld.wait(s)
    block.sync(f_sync)

    def f_g(g):
        ldg.dma(g.dma_start(out=p.Db[:, :], in_=p.Drow_d.unsqueeze(1).broadcast_to([1, 128, 16])))
        ldg.dma(g.dma_start(out=p.bout_b[:, :], in_=p.bout_d.unsqueeze(1).broadcast_to([1, 128, D])))
        ldg.wait(g)
    block.gpsimd(f_g)

    def f_sc(sc):
        ld.wait(sc)
        ldg.wait(sc)
        ev.inc(sc.activation(out=p.Acol[:, :], in_=p.alog[:, :], func=AF.Exp))
    block.scalar(f_sc)

    def f_v(v):
        ev.wait(v)
        v.tensor_scalar(out=p.Acol[:, :], in0=p.Acol[:, :], scalar1=-1.0,
                        scalar2=None, op0=ALU.mult)
    block.vector(f_v)
    barrier(p)



def emit_ln(p, s_in, wait_in, xt_v, xn_v, scr, cols, done_sem,
            nparts=128, war=None):
    block = p.block
    stat = p.lnstat
    mu = stat[0:nparts, cols:cols + 1]
    sq = stat[0:nparts, cols + 1:cols + 2]
    var = stat[0:nparts, cols + 2:cols + 3]
    rstd = stat[0:nparts, cols + 3:cols + 4]
    m2 = stat[0:nparts, cols + 4:cols + 5]
    if not hasattr(p, "_lnsems"):
        p._lnsems = (p.newsem("lnA", keep=True), p.newsem("lnB", keep=True),
                     p.newsem("lnC", keep=True), p.newsem("lnR", keep=True))
    sA, sB, sC, sR = p._lnsems

    def f_v1(v):
        s_in.wait(v, wait_in)
        sR.inc(v.reduce_sum(out=mu, in_=xt_v, axis=AXF))
        sR.wait(v)
        sR.inc(v.tensor_scalar(out=mu, in0=mu, scalar1=1.0 / D, scalar2=None, op0=ALU.mult))
        sR.wait(v)
        sR.inc(v.tensor_tensor(out=m2, in0=mu, in1=mu, op=ALU.mult))
        sR.wait(v)
    block.vector(f_v1)

    def f_s1(sc):
        s_in.wait(sc, wait_in)
        sA.inc(sc.activation(out=scr[0:nparts, :], in_=xt_v, func=AF.Square, accum_out=sq))
    block.scalar(f_s1)

    def f_v2(v):
        sA.wait(v)
        sR.inc(v.tensor_scalar(out=var, in0=sq, scalar1=1.0 / D, scalar2=None, op0=ALU.mult))
        sR.wait(v)
        sB.inc(v.tensor_tensor(out=var, in0=var, in1=m2, op=ALU.subtract))
    block.vector(f_v2)

    def f_s2(sc):
        sB.wait(sc)
        p.chA.inc(sc.activation(out=rstd, in_=var, func=AF.Ln, bias=p.cc[0:nparts, 0:1]))
        p.chA.wait(sc)
        sC.inc(sc.activation(out=rstd, in_=rstd, func=AF.Exp, scale=-0.5))
    block.scalar(f_s2)

    def f_v3(v):
        sC.wait(v)
        if war is not None:
            war[0].wait(v, war[1])
        done_sem.inc(v.tensor_scalar(out=xn_v, in0=xt_v, scalar1=mu, scalar2=rstd,
                                     op0=ALU.subtract, op1=ALU.mult))
    block.vector(f_v3)



def emit_mamba(p):
    nc, block = p.nc, p.block
    es = ExitStack()
    sb = lambda n, s, d: es.enter_context(nc.sbuf_tensor(n, s, d)).ap()
    es_a = ExitStack()
    sba = lambda n, s, d: es_a.enter_context(nc.sbuf_tensor(n, s, d)).ap()
    es_b = ExitStack()
    sbb = lambda n, s, d: es_b.enter_context(nc.sbuf_tensor(n, s, d)).ap()

    es_d = ExitStack()
    sbd = lambda n, s, d: es_d.enter_context(nc.sbuf_tensor(n, s, d)).ap()
    xb_fm = sbd("xb_fm", [128, 8 * T], BF16)
    xbT = sbd("xbT", [128, 8 * T], BF16)
    dtT = sbd("dtT", [128, NTT * 16], F32)
    acsT = sbd("acsT", [128, NTT * 16], F32)
    wTt = sbd("wTt", [128, NTT * 16], F32)
    cdb = sbd("cdb", [128, 128], F32)
    carry = sbd("carry", [128, 128], F32)
    carry_bf = sbd("carry_bf", [128, 128], BF16)
    ybuf = sbd("ybuf", [128, T], BF16)
    szbuf = sbd("szbuf", [128, T], BF16)
    gbuf = sbd("gbuf", [128, T], BF16)
    gscr = sbd("gscr", [128, T], F32)
    tfix = gscr
    g2buf = gscr.bitcast(BF16)[:, 0:T]
    rbbuf = sbd("rb", [128, T], BF16)
    ynbuf = sbd("ynb", [128, T], BF16)
    r1buf = sbd("r1row", [1, T], F32)
    dtrow = sbb("dtrow", [16, T], F32)
    xnf = sba("xnf", [128, NDC * W1], BF16)

    s_xp = (p.newsem("m_x0"), p.newsem("m_x1"))
    s_xh = p.newsem("m_xh")
    s_xn = p.newsem("m_xn")
    s_tr = p.newsem("m_tr")
    s_evt = p.newsem("m_evt")
    s_d1 = p.newsem("m_d1")

    with nc.sbuf_tensor("xtile", [128, 2 * D], F32) as _xt:
        xtile = _xt.ap()
        xntm = ybuf
        scr = szbuf
        xh = gscr[0:3, :]
        xnh = gbuf[0:3, 0:D]

        block.sync(lambda s: s_xh.dma(s.dma_start(out=xh[:, :], in_=p.xh_in[:, :])))

        for tt in range(NTT):
            pr = tt % 2
            xt_v = xtile[:, pr * D:(pr + 1) * D]

            sx = s_xp[tt % 2]

            def f_s(s, tt=tt, xt_v=xt_v, sx=sx):
                if tt >= 2:
                    s_xn.wait(s, tt - 1)
                    p._lnsems[0].wait(s, tt - 1)
                sx.dma(s.dma_start(out=xt_v, in_=p.x_in[tt * 128:(tt + 1) * 128, :]))
            block.sync(f_s)

            emit_ln(p, sx, sx.n, xt_v, xntm[:, :], scr, (tt % 2) * 5, s_xn,
                    war=(s_tr, 2 * tt))

            def f_t(t, tt=tt):
                s_xn.wait(t)
                s_evt.wait(t, 2 * tt)
                for half in range(2):
                    psum = p.psA if half == 0 else p.psB
                    ins = None
                    for q in range(8):
                        dch = half * 8 + q
                        ins = t.matmul(psum[:, q * 128:(q + 1) * 128],
                                       xntm[:, dch * 128:(dch + 1) * 128],
                                       p.ident[:, :], start=True, stop=True)
                    s_tr.inc(ins)
            block.tensor(f_t)

            def f_v4(v, tt=tt):
                for half in range(2):
                    psum = p.psA if half == 0 else p.psB
                    s_tr.wait(v, 2 * tt + half + 1)
                    ins = None
                    for q in range(8):
                        dch = half * 8 + q
                        o = dch * W1 + 3 + tt * 128
                        ins = v.tensor_copy(out=xnf[:, o:o + 128],
                                            in_=psum[:, q * 128:(q + 1) * 128])
                    s_evt.inc(ins)
            block.vector(f_v4)

        emit_ln(p, s_xh, 16, xh[:, :], xnh[:, :], scr, 10, s_xn, nparts=3,
                war=(s_tr, 2 * NTT))

        def f_th(t):
            s_xn.wait(t)
            s_evt.wait(t)
            ins = None
            for dch in range(16):
                ins = t.matmul(p.psA[:, dch * 8:dch * 8 + 3],
                               xnh[:, dch * 128:(dch + 1) * 128],
                               p.ident[0:3, 0:3], start=True, stop=True)
            s_tr.inc(ins)
        block.tensor(f_th)

        def f_vh(v):
            s_tr.wait(v)
            ins = None
            for dch in range(16):
                ins = v.tensor_copy(out=xnf[:, dch * W1:dch * W1 + 3],
                                    in_=p.psA[:, dch * 8:dch * 8 + 3])
            s_evt.inc(ins)
        block.vector(f_vh)

        if DBG_NAME == "xntm":
            s_dbg2 = p.newsem("dbgx2")

            def f_dbg2(g):
                s_xn.wait(g)
                s_dbg2.dma(g.dma_start(out=p.dbg[:, 0:2048], in_=xntm[:, :]))
                s_dbg2.dma(g.dma_start(out=p.dbg[:, 2048:2064], in_=p.lnstat[:, :]))
                s_dbg2.dma(g.dma_start(out=p.dbg[:, 2064:2192], in_=p.ident[:, :]))
                s_dbg2.dma(g.dma_start(out=p.dbg[:, 2192:2200], in_=p.cc[:, :]))
                s_dbg2.wait(g)
            block.gpsimd(f_dbg2)
        if DBG_NAME == "xnf":
            s_dbg = p.newsem("dbgx")

            def f_dbg(g):
                s_evt.wait(g)
                s_xn.wait(g)
                s_dbg.dma(g.dma_start(out=p.dbg[:, :], in_=xnf[:, 0:4096]))
                s_dbg.wait(g)
            block.gpsimd(f_dbg)
        barrier(p)
    if MAMBA_STEPS <= 1:
        es_a.close()
        es_b.close()
        es_d.close()
        es.close()
        return

    s_wlp = (p.newsem("ip_wl0", g=True), p.newsem("ip_wl1", g=True))
    s_mm = p.newsem("ip_mm")
    s_ev = p.newsem("ip_ev")
    s_sp = p.newsem("ip_sp")
    s_d2 = p.newsem("ip_d")
    WinT3 = p.WinT.rearrange("(c q) f -> q c f", q=128)

    with nc.sbuf_tensor("wst", [128, 2 * 2048], BF16) as _w, \
         nc.sbuf_tensor("evst", [128, 2048], BF16) as _e, \
         nc.sbuf_tensor("hst", [128, 4], BF16) as _h:
        wst, evst, hst = _w.ap(), _e.ap(), _h.ap()

        for f in range(41):
            pr = f % 2
            fw = 16 if f == 40 else 128
            w_v = wst[:, pr * 2048:pr * 2048 + 16 * fw]
            is_xbc = 16 <= f < 40
            ct = f - 16

            swl = s_wlp[f % 2]

            def f_g(g, f=f, w_v=w_v, fw=fw, swl=swl):
                if f >= 2:
                    s_d2.wait(g, f - 1)
                swl.dma(g.dma_start(
                    out=w_v.rearrange("q (c k) -> q c k", c=16),
                    in_=WinT3[:, :, f * 128:f * 128 + fw]))
            block.gpsimd(f_g)

            def f_t(t, f=f, w_v=w_v, fw=fw, is_xbc=is_xbc, swl=swl):
                swl.wait(t)
                s_evt.wait(t)
                s_xn.wait(t)
                s_d2.wait(t, f)
                for half in range(2):
                    psum = p.psA if half == 0 else p.psB
                    ins = None
                    for nb in range(2):
                        col = half * 1024 + nb * 512
                        for dch in range(16):
                            ins = t.matmul(
                                psum[0:fw, nb * 512:(nb + 1) * 512],
                                w_v[:, dch * fw:(dch + 1) * fw],
                                xnf[:, dch * W1 + 3 + col:dch * W1 + 3 + col + 512],
                                start=(dch == 0), stop=(dch == 15))
                    s_mm.inc(ins)
                if is_xbc:
                    ins = None
                    for dch in range(16):
                        ins = t.matmul(p.psC[:, 0:3], w_v[:, dch * 128:(dch + 1) * 128],
                                       xnf[:, dch * W1:dch * W1 + 3],
                                       start=(dch == 0), stop=(dch == 15))
                    s_mm.inc(ins)
            block.tensor(f_t)

            def f_ev(v, f=f, fw=fw, is_xbc=is_xbc):
                s_mm.wait(v)
                ins = None
                for half in range(2):
                    psum = p.psA if half == 0 else p.psB
                    if f == 40:
                        ins = v.tensor_copy(out=dtrow[:, half * 1024:(half + 1) * 1024],
                                            in_=psum[0:16, :])
                    else:
                        ins = v.tensor_copy(out=evst[:, half * 1024:(half + 1) * 1024],
                                            in_=psum[:, :])
                if is_xbc:
                    ins = v.tensor_copy(out=hst[:, 0:3], in_=p.psC[:, 0:3])
                s_ev.inc(ins)
            block.vector(f_ev)

            def f_st(s, f=f, is_xbc=is_xbc, ct=ct):
                s_ev.wait(s)
                if f == 40:
                    pass
                elif is_xbc:
                    s_sp.dma(s.dma_start(out=p.xbc_sp[ct, :, 3:3 + T], in_=evst[:, :]))
                    s_sp.dma(s.dma_start(out=p.xbc_sp[ct, :, 0:3], in_=hst[:, 0:3]))
                    s_sp.wait(s)
                else:
                    s_sp.dma(s.dma_start(out=p.z_sp[f, :, :], in_=evst[:, :]))
                    s_sp.wait(s)
                s_d2.inc(s.nop())
            block.sync(f_st)

        barrier(p)
    es_a.close()
    if MAMBA_STEPS <= 2:
        es_b.close()
        es_d.close()
        es.close()
        return

    s_clp = (p.newsem("cv_l0"), p.newsem("cv_l1"))
    s_cc = p.newsem("cv_c")
    s_ca = p.newsem("cv_a")
    s_ct = p.newsem("cv_t")
    s_ce = p.newsem("cv_ce")
    s_cv = p.newsem("cv_e")
    s_d3 = p.newsem("cv_d")

    with nc.sbuf_tensor("raw", [128, 2 * W1], BF16) as _r, \
         nc.sbuf_tensor("cacc", [128, T], F32) as _a, \
         nc.sbuf_tensor("cact", [128, T], BF16) as _c, \
         nc.sbuf_tensor("ctap", [128, T], F32) as _p:
        raw, cacc, cact, ctap = _r.ap(), _a.ap(), _c.ap(), _p.ap()

        for ct in range(24):
            pr = ct % 2
            r_v = raw[:, pr * W1:(pr + 1) * W1]

            scl = s_clp[ct % 2]

            def f_l(s, ct=ct, r_v=r_v, scl=scl):
                if ct >= 2:
                    s_cc.wait(s, ct - 1)
                scl.dma(s.dma_start(out=r_v, in_=p.xbc_sp[ct, :, :]))
            block.sync(f_l)

            def f_cv(v, ct=ct, r_v=r_v, scl=scl):
                scl.wait(v)
                s_ca.wait(v)
                v.tensor_scalar(out=cacc[:, :], in0=r_v[:, 0:T],
                                scalar1=p.convw[:, ct * 4:ct * 4 + 1], scalar2=None,
                                op0=ALU.mult)
                ins = None
                for k in range(1, 4):
                    v.tensor_scalar(out=ctap[:, :], in0=r_v[:, k:k + T],
                                    scalar1=p.convw[:, ct * 4 + k:ct * 4 + k + 1],
                                    scalar2=None, op0=ALU.mult)
                    ins = v.tensor_tensor(out=cacc[:, :], in0=cacc[:, :], in1=ctap[:, :],
                                          op=ALU.add)
                s_cc.inc(ins)
            block.vector(f_cv)

            def f_ca(sc, ct=ct):
                s_cc.wait(sc)
                s_ct.wait(sc)
                s_cv.wait(sc)
                s_ca.inc(sc.activation(out=cact[:, :], in_=cacc[:, :], func=AF.Silu,
                                       bias=p.convb[:, ct:ct + 1]))
            block.scalar(f_ca)

            if ct < 8:
                def f_cp(v, ct=ct):
                    s_ca.wait(v)
                    s_cv.inc(v.tensor_copy(out=xb_fm[:, ct * T:(ct + 1) * T],
                                           in_=cact[:, :]))
                block.vector(f_cp)

                def f_ct(t, ct=ct):
                    s_cv.wait(t)
                    s_ce.wait(t, 2 * ct)
                    for half in range(2):
                        psum = p.psA if half == 0 else p.psB
                        ins = None
                        for q in range(8):
                            tt = half * 8 + q
                            ins = t.matmul(psum[:, q * 128:(q + 1) * 128],
                                           cact[:, tt * 128:(tt + 1) * 128],
                                           p.ident[:, :], start=True, stop=True)
                        s_ct.inc(ins)
                block.tensor(f_ct)

                def f_ce(v, ct=ct):
                    for half in range(2):
                        psum = p.psA if half == 0 else p.psB
                        s_ct.wait(v, 2 * ct + half + 1)
                        ins = None
                        for q in range(8):
                            tt = half * 8 + q
                            ins = v.tensor_copy(
                                out=xbT[:, ct * T + tt * 128:ct * T + (tt + 1) * 128],
                                in_=psum[:, q * 128:(q + 1) * 128])
                        s_ce.inc(ins)
                block.vector(f_ce)
            else:
                def f_cs(s, ct=ct):
                    s_ca.wait(s)
                    s_cv.dma(s.dma_start(out=p.c_sp[ct - 8, :, :], in_=cact[:, :]))
                    s_cv.wait(s)
                block.sync(f_cs)

        barrier(p)
    if MAMBA_STEPS <= 3:
        es_b.close()
        es_d.close()
        es.close()
        return

    es_c = ExitStack()
    sbc = lambda n, s, d: es_c.enter_context(nc.sbuf_tensor(n, s, d)).ap()
    acsrow = sbc("acsrow", [16, T], F32)
    dsrow = sbc("dsrow", [16, T], F32)
    wrow = sbc("wrow", [16, T], F32)
    earow = sbc("earow", [16, T], BF16)
    cdf = sbc("cdf", [16, NCH], F32)
    s_p1 = p.newsem("dt_1")
    s_p2 = p.newsem("dt_2")
    s_p3 = p.newsem("dt_3")
    s_p4 = p.newsem("dt_4")
    s_p5 = p.newsem("dt_5")
    s_p6 = p.newsem("dt_6")

    def f_d1(v):
        s_p1.inc(v.tensor_scalar(out=dtrow[:, :], in0=dtrow[:, :],
                                 scalar1=p.dtb[:, 0:1], scalar2=None, op0=ALU.add))
    block.vector(f_d1)

    def f_d2(sc):
        s_p1.wait(sc)
        p.chA.inc(sc.activation(out=dsrow[:, :], in_=dtrow[:, :], func=AF.Abs))
        p.chA.wait(sc)
        p.chA.inc(sc.activation(out=dsrow[:, :], in_=dsrow[:, :], func=AF.Exp, scale=-1.0))
        p.chA.wait(sc)
        s_p2.inc(sc.activation(out=dsrow[:, :], in_=dsrow[:, :], func=AF.Ln, bias=p.cc[0:16, 1:2]))
    block.scalar(f_d2)

    def f_d3(v):
        s_p2.wait(v)
        v.tensor_scalar(out=wrow[:, :], in0=dtrow[:, :], scalar1=0.0, scalar2=None,
                        op0=ALU.max)
        v.tensor_tensor(out=dtrow[:, :], in0=wrow[:, :], in1=dsrow[:, :], op=ALU.add)
        s_p2b = p.newsem("dt_2b")
        s_p2b.inc(v.tensor_scalar(out=dsrow[:, :], in0=dtrow[:, :], scalar1=p.Acol[:, 0:1],
                                  scalar2=None, op0=ALU.mult))
        s_p2b.wait(v)
        ins = None
        for c in range(NCH):
            ins = v.tensor_tensor_scan(out=acsrow[:, c * 256:(c + 1) * 256],
                                       data0=p.ones16[:, :],
                                       data1=dsrow[:, c * 256:(c + 1) * 256],
                                       initial=0.0, op0=ALU.mult, op1=ALU.add)
        s_p3.inc(ins)
    block.vector(f_d3)

    def f_d4(sc):
        s_p3.wait(sc)
        for c in range(NCH):
            sc.activation(out=dsrow[:, c * 256:(c + 1) * 256],
                          in_=acsrow[:, c * 256:(c + 1) * 256], func=AF.Exp,
                          scale=-1.0, bias=acsrow[:, c * 256 + 255:c * 256 + 256])
        sc.activation(out=earow[:, :], in_=acsrow[:, :], func=AF.Exp)
        s_p4.inc(sc.activation(out=cdf[:, :],
                               in_=acsrow[:, 255:T:256], func=AF.Exp))
    block.scalar(f_d4)

    def f_d5(v):
        s_p4.wait(v)
        s_p5.inc(v.tensor_copy(out=wrow[:, :], in_=dsrow[:, :]))
    block.vector(f_d5)

    def f_d6(s):
        s_p5.wait(s)
        s_p6.dma(s.dma_start(out=p.acs_sp[:, :], in_=acsrow[:, :]))
        s_p6.dma(s.dma_start(out=p.eacs_sp[:, :], in_=earow[:, :]))
        s_p6.dma(s.dma_start(out=p.cd_sp[:, :], in_=cdf[:, :]))
        s_p6.wait(s)
    block.sync(f_d6)

    s_p6g = p.newsem("dt_6g", g=True)

    def f_d7(g):
        s_p6.wait(g)
        cdflat = p.cd_sp.rearrange("a b -> (a b)").unsqueeze(0)
        s_p6g.dma(g.dma_start(out=cdb[:, :],
                              in_=cdflat.unsqueeze(1).broadcast_to([1, 128, 128])))
        s_p6g.wait(g)
    block.gpsimd(f_d7)

    s_tt = p.newsem("dt_tt")
    s_te = p.newsem("dt_te")

    def f_d8(t):
        s_p5.wait(t)
        for src_i in range(3):
            for half in range(2):
                psum = p.psA if half == 0 else p.psB
                s_te.wait(t, max(0, 2 * src_i + half - 1))
                src = [dtrow, acsrow, wrow][src_i]
                bp = 0
                ins = None
                for q in range(8):
                    tt = half * 8 + q
                    ins = t.matmul(psum[:, q * 128:q * 128 + 16],
                                   src[:, tt * 128:(tt + 1) * 128],
                                   p.identf[0:16, 0:16], start=True, stop=True)
                s_tt.inc(ins)
    block.tensor(f_d8)

    def f_d9(v):
        for src_i in range(3):
            dst = [dtT, acsT, wTt][src_i]
            for half in range(2):
                psum = p.psA if half == 0 else p.psB
                s_tt.wait(v, 2 * src_i + half + 1)
                ins = None
                for q in range(8):
                    tt = half * 8 + q
                    ins = v.tensor_copy(out=dst[:, tt * 16:(tt + 1) * 16],
                                        in_=psum[:, q * 128:q * 128 + 16])
                s_te.inc(ins)
    block.vector(f_d9)

    barrier(p)
    es_c.close()
    es_b.close()
    if MAMBA_STEPS <= 4:
        es_d.close()
        es.close()
        return

    s_hlp = (p.newsem("sd_l0"), p.newsem("sd_l1"))
    s_hgp = (p.newsem("sd_g0", g=True), p.newsem("sd_g1", g=True))
    s_ci = p.newsem("sd_ci", g=True)
    s_g = p.newsem("sd_g")
    s_w = p.newsem("sd_w")
    s_lex = p.newsem("sd_le")
    s_y = p.newsem("sd_y")
    s_cu = p.newsem("sd_cu")
    s_ex = p.newsem("sd_ex", keep=True)
    p.s_ex = s_ex
    s_exd = p.newsem("sd_exd")
    s_gt = p.newsem("sd_gt")
    s_gtb = p.newsem("sd_gtb", g=True)
    s_d4 = p.newsem("sd_d")

    with nc.sbuf_tensor("chb", [128, 2 * T], BF16) as _ch, \
         nc.sbuf_tensor("blb", [128, 2 * T], F32) as _bl, \
         nc.sbuf_tensor("eab", [128, 2 * T], BF16) as _ea, \
         nc.sbuf_tensor("zhb", [128, 2 * T], BF16) as _zh, \
         nc.sbuf_tensor("segb", [128, 256], F32) as _sg, \
         nc.sbuf_tensor("lexb", [128, 512], BF16) as _lx, \
         nc.sbuf_tensor("gmb", [128, 512], BF16) as _gm, \
         nc.sbuf_tensor("wb", [128, 512], BF16) as _wb, \
         nc.sbuf_tensor("xdtb", [128, 256], BF16) as _xd, \
         nc.sbuf_tensor("bwb", [128, 256], BF16) as _bw, \
         nc.sbuf_tensor("cpb", [128, 256], BF16) as _cp:
        chb, blb, eab, zhb = _ch.ap(), _bl.ap(), _ea.ap(), _zh.ap()
        segb, lexb, gmb, wb = _sg.ap(), _lx.ap(), _gm.ap(), _wb.ap()
        xdtb, bwb, cpb = _xd.ap(), _bw.ap(), _cp.ap()

        for h in range(SSD_HEADS):
            pr = h % 2
            j = h // 4
            ch_v = chb[:, pr * T:(pr + 1) * T]
            bl_v = blb[:, pr * T:(pr + 1) * T]
            ea_v = eab[:, pr * T:(pr + 1) * T]
            zh_v = zhb[:, pr * T:(pr + 1) * T]

            shl = s_hlp[h % 2]
            shg = s_hgp[h % 2]

            def f_hl(s, h=h, ch_v=ch_v, zh_v=zh_v, shl=shl):
                if h >= 2:
                    s_d4.wait(s, h - 1)
                shl.dma(s.dma_start(out=ch_v, in_=p.c_sp[h, :, :]))
                shl.dma(s.dma_start(out=zh_v, in_=p.z_sp[h, :, :]))
            block.sync(f_hl)

            def f_hg(g, h=h, bl_v=bl_v, ea_v=ea_v):
                if h >= 2:
                    s_d4.wait(g, h - 1)
                shg.dma(g.dma_start(
                    out=bl_v, in_=p.acs_sp[h:h + 1, :].unsqueeze(1).broadcast_to([1, 128, T])))
                shg.dma(g.dma_start(
                    out=ea_v, in_=p.eacs_sp[h:h + 1, :].unsqueeze(1).broadcast_to([1, 128, T])))
                s_exd.wait(g, h)
                if SSD_POLL:
                    with g.register(f"meg{h}") as r:
                        g.reg_alu(r, g.compute_val(p.core), 2, ALU.mod)
                        with g.If_ne(r, 0):
                            if POLL_MODE >= 2:
                                poll_flag(p, g, FLAG_STATE0 + h)
                            s_ci.dma(g.dma_start(out=carry[:, :], in_=p.sh_state[h, :, :]))
                        with g.Else():
                            g.dma_start(out=carry[:, :], in_=p.zeros_d[:, :]).then_inc(s_ci.h, 16)
                else:
                    s_ci.dma(g.dma_start(out=carry[:, :], in_=p.zeros_d[:, :]))
            block.gpsimd(f_hg)

            def f_ci(v, h=h):
                s_ci.wait(v)
                s_cu.inc(v.tensor_copy(out=carry_bf[:, :], in_=carry[:, :]))
            block.vector(f_ci)
            cu_base = s_cu.n

            for c in range(NCH):
                def f_g1(t, h=h, c=c, j=j, ch_v=ch_v, shl=shl):
                    shl.wait(t)
                    if c == 0:
                        s_gt.wait(t)
                    s_w.wait(t)
                    ins = None
                    for k in range(2):
                        ins = t.matmul(p.psA[:, k * 256:(k + 1) * 256],
                                       xb_fm[:, (4 + j) * T + (c * 256 + k * 128):
                                             (4 + j) * T + (c * 256 + k * 128) + 128],
                                       ch_v[:, c * 256:(c + 1) * 256],
                                       start=True, stop=True)
                    s_g.inc(ins)
                block.tensor(f_g1)

                def f_w1(v, h=h, c=c, j=j, bl_v=bl_v, ea_v=ea_v, ch_v=ch_v, shg=shg):
                    s_g.wait(v)
                    shg.wait(v)
                    for k in range(2):
                        ttg = 2 * c + k
                        s_w.inc(v.tensor_scalar(
                            out=segb[:, :], in0=bl_v[:, c * 256:(c + 1) * 256],
                            scalar1=acsT[:, ttg * 16 + h:ttg * 16 + h + 1],
                            scalar2=0.0, op0=ALU.subtract, op1=ALU.min))
                        tri = p.triA if k == 0 else p.triB
                        v.tensor_tensor(out=gmb[:, k * 256:(k + 1) * 256],
                                        in0=p.psA[:, k * 256:(k + 1) * 256],
                                        in1=tri[:, :], op=ALU.mult)
                        s_lex.wait(v, 2 * (NCH * h + c) + k + 1)
                        v.tensor_tensor(out=wb[:, k * 256:(k + 1) * 256],
                                        in0=gmb[:, k * 256:(k + 1) * 256],
                                        in1=lexb[:, k * 256:(k + 1) * 256], op=ALU.mult)
                        v.tensor_scalar(out=xdtb[:, k * 128:(k + 1) * 128],
                                        in0=xbT[:, j * T + ttg * 128:j * T + (ttg + 1) * 128],
                                        scalar1=dtT[:, ttg * 16 + h:ttg * 16 + h + 1],
                                        scalar2=None, op0=ALU.mult)
                        v.tensor_scalar(out=bwb[:, k * 128:(k + 1) * 128],
                                        in0=xbT[:, (4 + j) * T + ttg * 128:
                                                (4 + j) * T + (ttg + 1) * 128],
                                        scalar1=wTt[:, ttg * 16 + h:ttg * 16 + h + 1],
                                        scalar2=None, op0=ALU.mult)
                    s_w.inc(v.tensor_tensor(out=cpb[:, :], in0=ch_v[:, c * 256:(c + 1) * 256],
                                            in1=ea_v[:, c * 256:(c + 1) * 256], op=ALU.mult))
                block.vector(f_w1)

                def f_le(sc, h=h, c=c):
                    for k in range(2):
                        s_w.wait(sc, 3 * (NCH * h + c) + k + 1)
                        s_lex.inc(sc.activation(out=lexb[:, k * 256:(k + 1) * 256],
                                                in_=segb[:, :], func=AF.Exp))
                block.scalar(f_le)

                def f_y1(t, h=h, c=c):
                    s_w.wait(t)
                    s_cu.wait(t, cu_base + c)
                    ins = t.matmul(p.psC[:, c * 256:(c + 1) * 256],
                                   xdtb[:, 0:128], wb[:, 0:256], start=True, stop=False)
                    ins = t.matmul(p.psC[:, c * 256:(c + 1) * 256],
                                   xdtb[:, 128:256], wb[:, 256:512], start=False, stop=False)
                    ins = t.matmul(p.psC[:, c * 256:(c + 1) * 256],
                                   carry_bf[:, :], cpb[:, :], start=False, stop=True)
                    if c > 0:
                        s_cu.wait(t, cu_base + c)
                    ins2 = t.matmul(p.psB[:, 0:128], bwb[:, 0:128], xdtb[:, 0:128],
                                    start=True, stop=False)
                    ins2 = t.matmul(p.psB[:, 0:128], bwb[:, 128:256], xdtb[:, 128:256],
                                    start=False, stop=True)
                    s_y.inc(ins2)
                block.tensor(f_y1)

                def f_cu(v, h=h, c=c):
                    s_y.wait(v)
                    v.scalar_tensor_tensor(out=carry[:, :], in0=carry[:, :],
                                           scalar=cdb[:, h * 8 + c:h * 8 + c + 1],
                                           in1=p.psB[:, 0:128],
                                           op0=ALU.mult, op1=ALU.add)
                    s_cu.inc(v.tensor_copy(out=carry_bf[:, :], in_=carry[:, :]))
                block.vector(f_cu)

            def f_ex(s, h=h):
                s_cu.wait(s)
                if SSD_POLL:
                    with s.register(f"mex{h}") as r:
                        s.reg_alu(r, s.compute_val(p.core), 2, ALU.mod)
                        with s.If_eq(r, 0):
                            s_ex.dma(s.dma_start(out=p.sh_state[h, :, :], in_=carry[:, :]))
                            s.wait_ge(s_ex.h, s_ex.n)
                            s_ex.dma(s.dma_start(out=p.sh_flg[FLAG_STATE0 + h, 0:1],
                                                 in_=p.fin[0:1, FLAG_STATE0 + h:FLAG_STATE0 + h + 1]))
                s_exd.inc(s.nop())
            block.sync(f_ex)

            def f_gt1(v, h=h, j=j):
                s_y.wait(v)
                v.tensor_scalar(out=tfix[:, :], in0=xb_fm[:, j * T:(j + 1) * T],
                                scalar1=p.Db[:, h:h + 1], scalar2=None, op0=ALU.mult)
                s_gt.inc(v.tensor_tensor(out=ybuf[:, :], in0=tfix[:, :], in1=p.psC[:, :],
                                         op=ALU.add))
            block.vector(f_gt1)

            def f_gt2(sc, h=h, zh_v=zh_v, shl=shl):
                shl.wait(sc)
                s_gt.wait(sc)
                s_gt.inc(sc.activation(out=szbuf[:, :], in_=zh_v, func=AF.Silu))
            block.scalar(f_gt2)

            def f_gt3(v, h=h):
                s_gt.wait(v)
                v.tensor_tensor(out=gbuf[:, :], in0=ybuf[:, :], in1=szbuf[:, :], op=ALU.mult)
                s_gt.inc(v.tensor_tensor(out=g2buf[:, :], in0=gbuf[:, :], in1=gbuf[:, :],
                                         op=ALU.mult))
            block.vector(f_gt3)

            def f_gt4(t, h=h):
                s_gt.wait(t)
                ins = None
                for nb in range(4):
                    ins = t.matmul(p.psC[0:1, nb * 512:(nb + 1) * 512],
                                   p.onescol[:, 0:1], g2buf[:, nb * 512:(nb + 1) * 512],
                                   start=True, stop=True)
                s_gt.inc(ins)
            block.tensor(f_gt4)

            def f_gt5(sc, h=h):
                s_gt.wait(sc)
                p.chA.inc(sc.activation(out=r1buf[:, :], in_=p.psC[0:1, :], func=AF.Ln,
                                        scale=1.0 / 128.0, bias=p.cc[0:1, 0:1]))
                p.chA.wait(sc)
                s_gt.inc(sc.activation(out=r1buf[:, :], in_=r1buf[:, :], func=AF.Exp,
                                       scale=-0.5))
            block.scalar(f_gt5)

            def f_gt6(s, h=h):
                s_gt.wait(s)
                s_gt.dma(s.dma_start(out=p.rstd_sp[h:h + 1, :], in_=r1buf[:, :]))
                s_gt.wait(s)
            block.sync(f_gt6)

            def f_gt7(g, h=h):
                s_gt.wait(g)
                s_gtb.dma(g.dma_start(
                    out=rbbuf[:, :],
                    in_=p.rstd_sp[h:h + 1, :].unsqueeze(1).broadcast_to([1, 128, T])))
            block.gpsimd(f_gt7)

            def f_gt8(v, h=h):
                s_gt.wait(v)
                s_gtb.wait(v)
                s_gt.inc(v.tensor_tensor(out=ynbuf[:, :], in0=gbuf[:, :], in1=rbbuf[:, :],
                                         op=ALU.mult))
            block.vector(f_gt8)

            def f_gt9(s, h=h):
                s_gt.wait(s)
                s_gt.dma(s.dma_start(out=p.yn_sp[h, :, :], in_=ynbuf[:, :]))
                s_gt.wait(s)
                s_d4.inc(s.nop())
            block.sync(f_gt9)

        barrier(p)
    es_d.close()
    if MAMBA_STEPS <= 5:
        es.close()
        return

    s_ol = p.newsem("op_l", g=True)
    s_on = p.newsem("op_n")
    s_oyp = (p.newsem("op_y0"), p.newsem("op_y1"))
    s_om = p.newsem("op_m")
    s_oe = p.newsem("op_e")
    s_d5 = p.newsem("op_d")

    with nc.sbuf_tensor("wot", [128, 16 * D], BF16) as _wo, \
         nc.sbuf_tensor("ynt", [128, 2 * D], BF16) as _yt, \
         nc.sbuf_tensor("xld", [128, 2 * D], F32) as _xl, \
         nc.sbuf_tensor("x1b", [128, D], F32) as _x1:
        wot, ynt, xld, x1b = _wo.ap(), _yt.ap(), _xl.ap(), _x1.ap()

        def f_wl(g):
            WoT3 = p.WoutT.rearrange("(c q) f -> q c f", q=128)
            s_ol.dma(g.dma_start(out=wot.rearrange("q (c k) -> q c k", c=16),
                                 in_=WoT3[:, :, :]))
        block.gpsimd(f_wl)

        def f_ws(v):
            s_ol.wait(v)
            ins = None
            for q in range(16):
                ins = v.tensor_scalar(out=wot[:, q * D:(q + 1) * D],
                                      in0=wot[:, q * D:(q + 1) * D],
                                      scalar1=p.normw[:, q:q + 1], scalar2=None,
                                      op0=ALU.mult)
            s_on.inc(ins)
        block.vector(f_ws)

        yn3 = p.yn_sp.rearrange("a b c -> a b c")

        for tt in range(NTT):
            pr = tt % 2
            yn_v = ynt[:, pr * D:(pr + 1) * D]
            xl_v = xld[:, pr * D:(pr + 1) * D]

            soy = s_oyp[tt % 2]

            def f_yl(s, tt=tt, yn_v=yn_v, xl_v=xl_v, soy=soy):
                if tt >= 2:
                    s_d5.wait(s, tt - 1)
                soy.dma(s.dma_start(
                    out=yn_v.rearrange("q (c k) -> q c k", c=16),
                    in_=p.yn_sp[:, :, tt * 128:(tt + 1) * 128].rearrange("c q k -> q c k")))
                soy.dma(s.dma_start(out=xl_v, in_=p.x_in[tt * 128:(tt + 1) * 128, :]))
            block.sync(f_yl)

            def f_om(t, tt=tt, yn_v=yn_v, soy=soy):
                s_on.wait(t)
                soy.wait(t)
                s_d5.wait(t, tt)
                ins = None
                for nb in range(4):
                    for q in range(16):
                        ins = t.matmul(p.psC[:, nb * 512:(nb + 1) * 512],
                                       yn_v[:, q * 128:(q + 1) * 128],
                                       wot[:, q * D + nb * 512:q * D + (nb + 1) * 512],
                                       start=(q == 0), stop=(q == 15))
                s_om.inc(ins)
            block.tensor(f_om)

            def f_oe(v, tt=tt, xl_v=xl_v):
                s_om.wait(v)
                s_oe.inc(v.tensor_tensor(out=x1b[:, :], in0=p.psC[:, :], in1=xl_v,
                                         op=ALU.add))
            block.vector(f_oe)

            def f_os(s, tt=tt):
                s_oe.wait(s)
                s_oe.dma(s.dma_start(out=p.x_res[tt * 128:(tt + 1) * 128, :],
                                     in_=x1b[:, :]))
                s_oe.wait(s)
                s_d5.inc(s.nop())
            block.sync(f_os)

        barrier(p)

    p.phase_release()
    es.close()




DQ_TILES = ([("g", a, nb) for a in range(8) for nb in range(8)]
            + [("u", a, nb) for a in range(8) for nb in range(8)]
            + [("d", b, 0) for b in range(64)])


def emit_dequant(p):
    nc, block = p.nc, p.block
    es = ExitStack()
    sb = lambda n, s, d: es.enter_context(nc.sbuf_tensor(n, s, d)).ap()
    ct = sb("dq_ct", [128, 2 * 2048], I32)
    scb = sb("dq_scb", [128, 2 * 2048], BF16)
    mI = sb("dq_m", [128, 2048], I32)
    sI = sb("dq_s", [128, 2048], I32)
    r4 = sb("dq_r4", [128, 2048], BF16)
    r6 = sb("dq_r6", [128, 2048], BF16)
    hB = sb("dq_h", [128, 2048], BF16)
    sg = sb("dq_sg", [128, 2048], BF16)
    t1 = sb("dq_t1", [128, 2048], BF16)
    wout = sb("dq_w", [128, 2 * 2 * 2048], BF16)

    s_ld = (p.newsem("dq_l0", g=True), p.newsem("dq_l1", g=True))
    s_a = p.newsem("dq_a")
    s_v = p.newsem("dq_v")
    s_st = p.newsem("dq_st")
    s_dn = p.newsem("dq_dn")

    half = len(DQ_TILES) // 2
    n_t = half

    def srcs(i, me_half):
        kind, a, nb = DQ_TILES[me_half * half + i] if False else DQ_TILES[i]
        return kind, a, nb

    for i in range(len(DQ_TILES)):
        kind, a, nb = DQ_TILES[i]
        pr = i % 2
        ct_v = ct[:, pr * 2048:(pr + 1) * 2048]
        scb_v = scb[:, pr * 2048:(pr + 1) * 2048]
        w_v = wout[:, pr * 4096:(pr + 1) * 4096]
        if kind == "g":
            psrc = p.g_pT[a * 128:(a + 1) * 128, nb * 2048:(nb + 1) * 2048]
            ssrc = p.g_scT[8 * a:8 * a + 8, nb * 2048:(nb + 1) * 2048]
            dlo = p.sh_wg[2 * a, :, nb * 2048:(nb + 1) * 2048]
            dhi = p.sh_wg[2 * a + 1, :, nb * 2048:(nb + 1) * 2048]
        elif kind == "u":
            psrc = p.u_pT[a * 128:(a + 1) * 128, nb * 2048:(nb + 1) * 2048]
            ssrc = p.u_scT[8 * a:8 * a + 8, nb * 2048:(nb + 1) * 2048]
            dlo = p.sh_wu[2 * a, :, nb * 2048:(nb + 1) * 2048]
            dhi = p.sh_wu[2 * a + 1, :, nb * 2048:(nb + 1) * 2048]
        else:
            psrc = p.d_pT[a * 128:(a + 1) * 128, :]
            ssrc = p.d_scT[8 * a:8 * a + 8, :]
            dlo = p.sh_wd[2 * a, :, :]
            dhi = p.sh_wd[2 * a + 1, :, :]

        def f_l(g, i=i, ct_v=ct_v, scb_v=scb_v, psrc=psrc, ssrc=ssrc, pr=pr):
            if i >= 2:
                s_v.wait(g, s_v.base + 4 * i - 4)
            s_ld[pr].dma(g.dma_start(out=ct_v, in_=psrc))
            s_ld[pr].dma(g.dma_start(
                out=scb_v, in_=ssrc.unsqueeze(1).broadcast_to([8, 16, 2048])))
        block.gpsimd(f_l)

        for half_i in range(2):
            w_half = w_v[:, half_i * 2048:(half_i + 1) * 2048]

            def f_v1(v, i=i, half_i=half_i, ct_v=ct_v, pr=pr):
                s_ld[pr].wait(v)
                s_a.wait(v, s_a.base + 2 * i + half_i)
                if half_i == 0:
                    v.tensor_scalar(out=mI[:, :], in0=ct_v, scalar1=7, scalar2=None,
                                    op0=ALU.bitwise_and)
                    s_v.inc(v.tensor_scalar(out=sI[:, :], in0=ct_v, scalar1=3, scalar2=1,
                                            op0=ALU.logical_shift_right, op1=ALU.bitwise_and))
                else:
                    v.tensor_scalar(out=mI[:, :], in0=ct_v, scalar1=4, scalar2=7,
                                    op0=ALU.logical_shift_right, op1=ALU.bitwise_and)
                    s_v.inc(v.tensor_scalar(out=sI[:, :], in0=ct_v, scalar1=7, scalar2=1,
                                            op0=ALU.logical_shift_right, op1=ALU.bitwise_and))
            block.vector(f_v1)

            def f_a1(sc, i=i, half_i=half_i):
                s_v.wait(sc)
                sc.activation(out=r4[:, :], in_=mI[:, :], func=AF.Relu, scale=0.5,
                              bias=p.cc[:, 2:3])
                sc.activation(out=r6[:, :], in_=mI[:, :], func=AF.Relu,
                              bias=p.cc[:, 3:4])
                s_a.inc(sc.activation(out=hB[:, :], in_=mI[:, :], func=AF.Copy, scale=0.5))
            block.scalar(f_a1)

            def f_v2(v, i=i, half_i=half_i, w_half=w_half, pr=pr):
                s_a.wait(v)
                if half_i == 0 and i >= 2:
                    s_st.wait(v, s_st.base + 32 * (i - 1))
                v.tensor_scalar(out=sg[:, :], in0=sI[:, :], scalar1=-2.0, scalar2=1.0,
                                op0=ALU.mult, op1=ALU.add)
                v.tensor_tensor(out=t1[:, :], in0=hB[:, :], in1=r4[:, :], op=ALU.add)
                v.tensor_tensor(out=t1[:, :], in0=t1[:, :], in1=r6[:, :], op=ALU.add)
                v.tensor_tensor(out=t1[:, :], in0=t1[:, :], in1=sg[:, :], op=ALU.mult)
                s_v.inc(v.tensor_tensor(out=w_half, in0=t1[:, :],
                                        in1=scb[:, pr * 2048:(pr + 1) * 2048],
                                        op=ALU.mult))
            block.vector(f_v2)

        def f_st(s, i=i, w_v=w_v, dlo=dlo, dhi=dhi):
            s_v.wait(s)
            s_st.dma(s.dma_start(out=dlo, in_=w_v[:, 0:2048]))
            s_st.dma(s.dma_start(out=dhi, in_=w_v[:, 2048:4096]))
        block.sync(f_st)

    def f_fin(s):
        s_st.wait(s)
    block.sync(f_fin)

    barrier(p)
    p.phase_release()
    es.close()



def emit_ffn(p, call):
    nc, block = p.nc, p.block
    es = ExitStack()
    sb = lambda n, s, d: es.enter_context(nc.sbuf_tensor(n, s, d)).ap()
    TP = 1024
    NTP = 8

    xacc = sb(f"ff{call}_xacc", [128, NTP * D], F32)
    xnff = sb(f"ff{call}_xnf", [128, 16 * TP], BF16)
    inter = sb(f"ff{call}_int", [128, 8 * TP], BF16)
    wg = sb(f"ff{call}_wg", [128, 2 * 4096], BF16)
    wu = sb(f"ff{call}_wu", [128, 2 * 4096], BF16)
    wd = sb(f"ff{call}_wd", [128, 8 * 2048], BF16)
    gsb = sb(f"ff{call}_g", [128, TP], BF16)
    uh = sb(f"ff{call}_u", [128, TP], BF16)
    ga = sb(f"ff{call}_a", [128, TP], BF16)
    gq = sb(f"ff{call}_q", [128, TP], BF16)
    gc = sb(f"ff{call}_c", [128, TP], BF16)
    xn_tm = sb(f"ff{call}_xn", [128, D], F32)
    scr = inter[:, 0:D]

    s_x = (p.newsem(f"f{call}_x0"), p.newsem(f"f{call}_x1"))
    s_xn = p.newsem(f"f{call}_xn")
    s_tr = p.newsem(f"f{call}_tr")
    s_evt = p.newsem(f"f{call}_ev")
    s_wl = (p.newsem(f"f{call}_w0"), p.newsem(f"f{call}_w1"))
    s_dl = (p.newsem(f"f{call}_d0"), p.newsem(f"f{call}_d1"))
    s_gu = p.newsem(f"f{call}_gu")
    s_ge = p.newsem(f"f{call}_ge")
    s_iv = p.newsem(f"f{call}_iv")
    s_dm = p.newsem(f"f{call}_dm")
    s_de = p.newsem(f"f{call}_de")
    s_gd = p.newsem(f"f{call}_gd")
    s_sp = p.newsem(f"f{call}_sp")
    s_pd = p.newsem(f"f{call}_pd")


    for ps in range(2):
        def emit_tile(tt, ps, load=True):
            gtt = ps * NTP + tt
            xv = xacc[:, tt * D:(tt + 1) * D]
            sx = s_x[tt % 2]

            def f_xl(s, gtt=gtt, xv=xv, sx=sx, ps=ps, tt=tt):
                if ps > 0 and tt == 0:
                    s_sp.wait(s)
                sx.dma(s.dma_start(out=xv, in_=p.x_res[gtt * 128:(gtt + 1) * 128, :]))
            if load:
                block.sync(f_xl)

            emit_ln(p, sx, None, xv, xn_tm[:, :], scr, (tt % 2) * 5, s_xn,
                    war=(s_tr, p.ph.n * 0 + s_tr.n))

            if ps == 0 and tt == 0 and call == 1 and DBG_NAME in ("ffnxn", "ffnxv"):
                s_dsn = p.newsem("fdbgxn")

                def f_snap(v):
                    s_xn.wait(v)
                    s_dsn.inc(v.tensor_copy(out=wd[:, 0:2048], in_=xn_tm[:, :]))
                    s_dsn.inc(v.tensor_copy(out=wd[:, 2048:4096], in_=xv))
                    s_dsn.inc(v.tensor_copy(out=wd[:, 4096:4112],
                                            in_=p.lnstat[:, :]))
                block.vector(f_snap)

                def f_dsn(g):
                    s_dsn.wait(g)
                    s_dsn.dma(g.dma_start(out=p.dbg[:, 0:2048],
                                          in_=wd[:, 2048:4096]))
                    s_dsn.dma(g.dma_start(out=p.dbg[:, 2048:2064],
                                          in_=wd[:, 4096:4112]))
                    s_dsn.dma(g.dma_start(out=p.dbg[:, 2064:3088],
                                          in_=wd[:, 0:1024]))
                    s_dsn.wait(g)
                block.gpsimd(f_dsn)

            def f_tr(t, tt=tt):
                s_xn.wait(t)
                s_evt.wait(t)
                for half in range(2):
                    psum = p.psA if half == 0 else p.psB
                    ins = None
                    for qq in range(8):
                        q = half * 8 + qq
                        a2, par = q // 2, q % 2
                        ins = t.matmul(psum[:, qq * 128:(qq + 1) * 128],
                                       xn_tm[:, a2 * 256 + par:a2 * 256 + 256:2],
                                       p.identf[:, :], start=True, stop=True)
                    s_tr.inc(ins)
            block.tensor(f_tr)

            def f_te(v, tt=tt):
                ins = None
                for half in range(2):
                    psum = p.psA if half == 0 else p.psB
                    s_tr.wait(v)
                    for qq in range(8):
                        q = half * 8 + qq
                        ins = v.tensor_copy(out=xnff[:, q * TP + tt * 128:
                                                     q * TP + (tt + 1) * 128],
                                            in_=psum[:, qq * 128:(qq + 1) * 128])
                s_evt.inc(ins)
            block.vector(f_te)

        if ps == 0:
            def f_prime(t):
                ins = None
                for qq in range(8):
                    ins = t.matmul(p.psA[:, qq * 128:(qq + 1) * 128],
                                   p.identf[:, :], p.identf[:, :],
                                   start=True, stop=True)
                    ins = t.matmul(p.psB[:, qq * 128:(qq + 1) * 128],
                                   p.identf[:, :], p.identf[:, :],
                                   start=True, stop=True)
                s_tr.inc(ins)
            block.tensor(f_prime)

            def f_prime2(v):
                s_tr.wait(v)
                s_evt.inc(v.tensor_copy(out=xn_tm[:, 0:1024], in_=p.psA[:, :]))
            block.vector(f_prime2)

        for tt in range(NTP):
            emit_tile(tt, ps)

        if ps == 0 and DBG_NAME == "ffnx0" and call == 1:
            s_dx0 = p.newsem(f"f{call}dbgx0")

            def f_dbgx0(g):
                s_evt.wait(g)
                s_dx0.dma(g.dma_start(out=p.dbg[:, 0:2048], in_=xnff[:, 0:2048]))
                s_dx0.dma(g.dma_start(out=p.dbg[:, 2048:4096],
                                      in_=xnff[:, TP:TP + 2048]))
                s_dx0.wait(g)
            block.gpsimd(f_dbgx0)

        if ps == 0 and DBG_NAME == "ffnw" and call == 1:
            s_dw = p.newsem(f"f{call}dbgw")

            def f_dbgw(g):
                s_wl[0].wait(g, s_wl[0].base + 32)
                s_dw.dma(g.dma_start(out=p.dbg[:, :], in_=wg[:, 0:4096]))
                s_dw.wait(g)
            block.gpsimd(f_dbgw)

        for grp in range(16):
            for mc in range(8):
                mchunk = grp * 8 + mc
                b2, par = mchunk // 2, mchunk % 2
                prw = b2 % 2
                wg_v = wg[:, prw * 4096:(prw + 1) * 4096]
                wu_v = wu[:, prw * 4096:(prw + 1) * 4096]
                wd_v = wd[:, mc * 2048:(mc + 1) * 2048]

                if par == 0:
                    def f_wl(s, b2=b2, wg_v=wg_v, wu_v=wu_v, mchunk=mchunk):
                        swl = s_wl[(b2) % 2]
                        if mchunk >= 2:
                            s_gu.wait(s)
                        swl.dma(s.dma_start(
                            out=wg_v.rearrange("p (c k) -> p c k", c=16),
                            in_=p.sh_wg[:, :, 256 * b2:256 * b2 + 256].rearrange(
                                "c p k -> p c k")))
                        swl.dma(s.dma_start(
                            out=wu_v.rearrange("p (c k) -> p c k", c=16),
                            in_=p.sh_wu[:, :, 256 * b2:256 * b2 + 256].rearrange(
                                "c p k -> p c k")))
                    block.sync(f_wl)

                def f_dl(s, mchunk=mchunk, mc=mc, wd_v=wd_v, dmn=None):
                    sdl = s_dl[mchunk % 2]
                    s_dm.wait(s)
                    sdl.dma(s.dma_start(out=wd_v, in_=p.sh_wd[mchunk, :, :]))
                block.sync(f_dl)

                def f_gu(t, mchunk=mchunk, mc=mc, b2=b2, par=par, wg_v=wg_v, wu_v=wu_v):
                    s_wl[b2 % 2].wait(t)
                    s_evt.wait(t)
                    s_ge.wait(t)
                    if mc == 0:
                        s_dm.wait(t)
                    ins = None
                    for w_i, (wv, psum) in enumerate([(wg_v, p.psA), (wu_v, p.psB)]):
                        for nb in range(2):
                            for q in range(16):
                                ins = t.matmul(
                                    psum[:, nb * 512:(nb + 1) * 512],
                                    wv[:, q * 256 + par:q * 256 + 256:2],
                                    xnff[:, q * TP + nb * 512:q * TP + (nb + 1) * 512],
                                    start=(q == 0), stop=(q == 15))
                    s_gu.inc(ins)
                block.tensor(f_gu)

                def f_ge(sc, mc=mc):
                    s_gu.wait(sc)
                    s_iv.wait(sc)
                    sc.activation(out=gsb[:, :], in_=p.psA[:, :], func=AF.Copy)
                    s_ge.inc(sc.activation(out=uh[:, :], in_=p.psB[:, :], func=AF.Copy,
                                           scale=0.5))
                block.scalar(f_ge)

                if ps == 0 and grp == 0 and mc == 0 and call == 1 and DBG_NAME == "ffng2":
                    s_dg3 = p.newsem("fdbg3")

                    def f_dbg3(g):
                        s_ge.wait(g)
                        s_dg3.dma(g.dma_start(out=p.int_sp[:, 0:4096], in_=xnff[:, 0:4096]))
                        s_dg3.dma(g.dma_start(out=p.int_sp[:, 4096:4112], in_=p.lnstat[:, :]))
                        s_dg3.dma(g.dma_start(out=p.int_sp[:, 4112:6160],
                                              in_=xacc[:, 0:2048]))
                        s_dg3.wait(g)
                    block.gpsimd(f_dbg3)
                if ps == 0 and grp == 0 and mc == 0 and call == 1 and DBG_NAME == "ffng":
                    s_dg2 = p.newsem("fdbgg")

                    def f_dbgg(g, wg_v=wg_v):
                        s_ge.wait(g)
                        s_dg2.dma(g.dma_start(out=p.int_sp[:, 0:1024], in_=gsb[:, :]))
                        s_dg2.dma(g.dma_start(out=p.int_sp[:, 1024:2048], in_=uh[:, :]))
                        s_dg2.dma(g.dma_start(out=p.int_sp[:, 2048:3072],
                                              in_=xnff[:, 0:1024]))
                        s_dg2.dma(g.dma_start(out=p.int_sp[:, 3072:4096],
                                              in_=xnff[:, 1 * 1024:1 * 1024 + 1024]))
                        s_dg2.wait(g)
                    block.gpsimd(f_dbgg)

                def f_gv(v, mc=mc):
                    s_ge.wait(v)
                    v.tensor_tensor(out=ga[:, :], in0=gsb[:, :], in1=gsb[:, :], op=ALU.mult)
                    v.tensor_scalar(out=ga[:, :], in0=ga[:, :], scalar1=0.035677408136,
                                    scalar2=0.7978845608, op0=ALU.mult, op1=ALU.add)
                    s_iv.inc(v.tensor_tensor(out=gq[:, :], in0=ga[:, :], in1=gsb[:, :],
                                             op=ALU.mult))
                block.vector(f_gv)

                def f_gt(sc, mc=mc):
                    s_iv.wait(sc)
                    s_iv.inc(sc.activation(out=gq[:, :], in_=gq[:, :], func=AF.Tanh))
                block.scalar(f_gt)

                def f_gv2(v, mc=mc):
                    s_iv.wait(v)
                    v.tensor_tensor(out=gc[:, :], in0=gsb[:, :], in1=uh[:, :], op=ALU.mult)
                    s_iv.inc(v.scalar_tensor_tensor(
                        out=inter[:, mc * TP:(mc + 1) * TP], in0=gq[:, :], scalar=1.0,
                        in1=gc[:, :], op0=ALU.add, op1=ALU.mult))
                block.vector(f_gv2)

            if ps == 0 and grp == 0 and call == 1 and DBG_NAME == "ffni":
                s_di = p.newsem("fdbgi")

                def f_dbgi(s):
                    s_iv.wait(s)
                    s_di.dma(s.dma_start(out=p.int_sp[:, :], in_=inter[:, :]))
                    s_di.wait(s)
                block.sync(f_dbgi)

            de_base = s_de.n
            dm_base = s_dm.n

            def f_dn(t, grp=grp, de_base=de_base):
                s_iv.wait(t)
                s_dl[0].wait(t)
                s_dl[1].wait(t)
                for tt8 in range(NTP):
                    s_de.wait(t, de_base + tt8)
                    ins = None
                    for mc in range(8):
                        for nb in range(4):
                            ins = t.matmul(
                                p.psC[:, nb * 512:(nb + 1) * 512],
                                inter[:, mc * TP + tt8 * 128:mc * TP + (tt8 + 1) * 128],
                                wd[:, mc * 2048 + nb * 512:mc * 2048 + (nb + 1) * 512],
                                start=(mc == 0), stop=(mc == 7))
                    s_dm.inc(ins)
            block.tensor(f_dn)

            def f_dne(v, grp=grp, dm_base=dm_base):
                for tt8 in range(NTP):
                    s_dm.wait(v, dm_base + tt8 + 1)
                    s_de.inc(v.tensor_tensor(
                        out=xacc[:, tt8 * D:(tt8 + 1) * D],
                        in0=xacc[:, tt8 * D:(tt8 + 1) * D],
                        in1=p.psC[:, :], op=ALU.add))
            block.vector(f_dne)

        def f_wb(s, ps=ps):
            s_de.wait(s)
            ins = None
            for tt in range(NTP):
                gtt = ps * NTP + tt
                s_sp.dma(s.dma_start(out=p.x_res[gtt * 128:(gtt + 1) * 128, :],
                                     in_=xacc[:, tt * D:(tt + 1) * D]))
            s_sp.wait(s)
        block.sync(f_wb)

    barrier(p)
    p.phase_release()
    es.close()


def emit_attn(p):
    nc, block = p.nc, p.block
    NQB = 4
    NKC = 32
    NTL = 16
    SCALE = 0.08838834764831845

    es_q = ExitStack()
    qT = es_q.enter_context(nc.sbuf_tensor("at_qT", [128, AH * T], BF16)).ap()
    es_1 = ExitStack()
    sb1 = lambda n, s, d: es_1.enter_context(nc.sbuf_tensor(n, s, d)).ap()
    xnT = sb1("at_xnT", [128, 16 * T], BF16)
    ascr = sb1("at_scr", [128, D], BF16)
    bv_b = sb1("at_bv", [128, D], F32)
    vtmp = ascr
    es_ln = ExitStack()
    sbl = lambda n, s, d: es_ln.enter_context(nc.sbuf_tensor(n, s, d)).ap()
    xr = sbl("at_xr", [128, 2 * D], F32)
    xn_tm = sbl("at_xntm", [128, D], F32)

    s_x = (p.newsem("a_x0"), p.newsem("a_x1"))
    s_xn = p.newsem("a_xn")
    s_tr = p.newsem("a_tr")
    s_evt = p.newsem("a_evt")
    s_bv = p.newsem("a_bv", g=True)
    s_wq = (p.newsem("a_wq0", g=True), p.newsem("a_wq1", g=True))
    s_qk = p.newsem("a_qk")
    s_qe = p.newsem("a_qe")
    s_ks = p.newsem("a_ks")
    s_wv = (p.newsem("a_wv0", g=True), p.newsem("a_wv1", g=True))
    s_vm = p.newsem("a_vm")
    s_ve = p.newsem("a_ve")
    s_vs = p.newsem("a_vs")
    s_fl = p.newsem("a_fl")
    s_kvf = p.newsem("a_kvf")
    s_kv = (p.newsem("a_kv0"), p.newsem("a_kv1"))
    s_sm = p.newsem("a_sm")
    s_pe = p.newsem("a_pe")
    s_pv = p.newsem("a_pv")
    s_rc = p.newsem("a_rc")
    s_dsp = p.newsem("a_dsp")
    s_db = (p.newsem("a_db0"), p.newsem("a_db1"))
    s_ye = p.newsem("a_ye")
    s_wo = p.newsem("a_wo", g=True)
    s_om = p.newsem("a_om")
    s_oe = p.newsem("a_oe")
    s_xr2 = (p.newsem("a_xr20"), p.newsem("a_xr21"))
    s_ob = p.newsem("a_ob")

    def f_bv(g):
        s_bv.dma(g.dma_start(out=bv_b[:, :],
                             in_=p.bqkv_vr.unsqueeze(1).broadcast_to([1, 128, D])))
    block.gpsimd(f_bv)

    for tt in range(NTL):
        pr = tt % 2
        xv = xr[:, pr * D:(pr + 1) * D]
        sxp = s_x[pr]

        def f_xl(s, tt=tt, xv=xv, sxp=sxp):
            if tt >= 2:
                s_xn.wait(s, s_xn.base + tt - 1)
            sxp.dma(s.dma_start(out=xv, in_=p.x_res[tt * 128:(tt + 1) * 128, :]))
        block.sync(f_xl)

        emit_ln(p, sxp, None, xv, xn_tm[:, :], ascr, pr * 5, s_xn,
                war=(s_tr, s_tr.n))

        def f_tr(t, tt=tt):
            s_xn.wait(t)
            s_evt.wait(t)
            for half in range(2):
                psum = p.psA if half == 0 else p.psB
                ins = None
                for qq in range(8):
                    q = half * 8 + qq
                    a2, par = q // 2, q % 2
                    ins = t.matmul(psum[:, qq * 128:(qq + 1) * 128],
                                   xn_tm[:, a2 * 256 + par:a2 * 256 + 256:2],
                                   p.identf[:, :], start=True, stop=True)
                s_tr.inc(ins)
        block.tensor(f_tr)

        def f_te(v, tt=tt):
            ins = None
            for half in range(2):
                psum = p.psA if half == 0 else p.psB
                s_tr.wait(v)
                for qq in range(8):
                    q = half * 8 + qq
                    ins = v.tensor_copy(out=xnT[:, q * T + tt * 128:
                                                q * T + (tt + 1) * 128],
                                        in_=psum[:, qq * 128:(qq + 1) * 128])
            s_evt.inc(ins)
        block.vector(f_te)

    es_ln.close()
    es_k = ExitStack()
    sbk = lambda n, s, d: es_k.enter_context(nc.sbuf_tensor(n, s, d)).ap()
    wqk = sbk("at_wqk", [128, 2 * 2048], BF16)
    kT_sb = sbk("at_kt", [128, 2 * 2048], BF16)

    def emit_qk_chunk(idx, dc, wqkb):
        parw = idx % 2
        wbuf = wqkb[:, parw * 2048:(parw + 1) * 2048]

        def f_wl(g, idx=idx, dc=dc, wbuf=wbuf):
            swq = s_wq[parw]
            if idx == 0:
                s_xn.wait(g, s_xn.base + NTL)
                s_tr.wait(g, s_tr.base + 2 * NTL)
            if idx == 16:
                s_vm.wait(g, s_vm.base + 64)
            if idx >= 2:
                s_qk.wait(g, s_qk.base + (idx - 1) * 4)
            for q in range(16):
                a2, par = q // 2, q % 2
                swq.dma(g.dma_start(
                    out=wbuf[:, q * 128:(q + 1) * 128],
                    in_=p.wqkv[a2 * 256 + par:a2 * 256 + 256:2,
                               dc * 128:(dc + 1) * 128]))
        block.gpsimd(f_wl)

        def f_mm(t, idx=idx, wbuf=wbuf):
            s_wq[parw].wait(t)
            if idx == 0:
                s_evt.wait(t, s_evt.base + NTL)
            if idx == 16:
                s_ve.wait(t, s_ve.base + 64)
            for tb in range(4):
                n2 = idx * 4 + tb
                psv = p.psA[:, (tb % 2) * 512:(tb % 2 + 1) * 512]
                if n2 >= 2:
                    s_qe.wait(t, s_qe.base + n2 - 1)
                ins = None
                for q in range(16):
                    ins = t.matmul(psv, wbuf[:, q * 128:(q + 1) * 128],
                                   xnT[:, q * T + tb * 512:q * T + (tb + 1) * 512],
                                   start=(q == 0), stop=(q == 15))
                s_qk.inc(ins)
        block.tensor(f_mm)

        def f_ev(v, idx=idx, dc=dc):
            for tb in range(4):
                n2 = idx * 4 + tb
                s_qk.wait(v, s_qk.base + n2 + 1)
                if dc >= 16 and tb == 0 and idx >= 2:
                    s_ks.wait(v, s_ks.base + (idx - 1) * 16)
                psv = p.psA[:, (tb % 2) * 512:(tb % 2 + 1) * 512]
                if dc < 16:
                    dst = qT[:, dc * T + tb * 512:dc * T + (tb + 1) * 512]
                else:
                    dst = kT_sb[:, parw * 2048 + tb * 512:parw * 2048 + (tb + 1) * 512]
                s_qe.inc(v.tensor_scalar(out=dst, in0=psv,
                                         scalar1=p.bqkv[:, dc:dc + 1],
                                         scalar2=None, op0=ALU.add))
        block.vector(f_ev)

        if dc >= 16:
            def f_ks(s, idx=idx, dc=dc):
                s_qe.wait(s, s_qe.base + (idx + 1) * 4)
                with s.register(f"akr{idx}") as r:
                    s.reg_alu(r, s.compute_val(p.core), 2, ALU.mod)
                    with s.If_eq(r, 0):
                        s.dma_start(out=p.sh_k[0, dc - 16],
                                    in_=kT_sb[:, parw * 2048:(parw + 1) * 2048]
                                    ).then_inc(s_ks.h, 16)
                    with s.If_ne(r, 0):
                        s.dma_start(out=p.sh_k[1, dc - 16],
                                    in_=kT_sb[:, parw * 2048:(parw + 1) * 2048]
                                    ).then_inc(s_ks.h, 16)
                s_ks.n += 16
            block.sync(f_ks)

    for idx in range(16):
        emit_qk_chunk(idx, 16 + idx, wqk)
    es_k.close()

    es_v = ExitStack()
    wv = es_v.enter_context(nc.sbuf_tensor("at_wv", [128, 2 * 16 * 512], BF16)).ap()
    for vb in range(4):
        parv = vb % 2

        def f_wvl(g, vb=vb, parv=parv):
            swv = s_wv[parv]
            if vb == 0:
                s_qk.wait(g, s_qk.base + 64)
                s_ks.wait(g, s_ks.base + 256)
            if vb >= 2:
                s_vm.wait(g, s_vm.base + (vb - 1) * 16)
            for q in range(16):
                a2, par = q // 2, q % 2
                swv.dma(g.dma_start(
                    out=wv[:, parv * 8192 + q * 512:parv * 8192 + (q + 1) * 512],
                    in_=p.wqkv[a2 * 256 + par:a2 * 256 + 256:2,
                               2 * D + vb * 512:2 * D + (vb + 1) * 512]))
        block.gpsimd(f_wvl)

        for tile in range(NTL):
            n3 = vb * 16 + tile
            grp = n3 // 4

            def f_vm(t, n3=n3, parv=parv, tile=tile, vb=vb):
                s_wv[parv].wait(t)
                if n3 == 0:
                    s_qe.wait(t, s_qe.base + 64)
                if n3 >= 2:
                    s_ve.wait(t, s_ve.base + n3 - 1)
                psv = p.psA[:, (n3 % 2) * 512:(n3 % 2 + 1) * 512]
                ins = None
                for q in range(16):
                    ins = t.matmul(psv, xnT[:, q * T + tile * 128:q * T + (tile + 1) * 128],
                                   wv[:, parv * 8192 + q * 512:parv * 8192 + (q + 1) * 512],
                                   start=(q == 0), stop=(q == 15))
                s_vm.inc(ins)
            block.tensor(f_vm)

            def f_ve(v, n3=n3, vb=vb, tile=tile, grp=grp):
                if n3 == 0:
                    s_bv.wait(v)
                s_vm.wait(v, s_vm.base + n3 + 1)
                if tile % 4 == 0 and grp >= 1:
                    s_vs.wait(v, s_vs.base + grp * 256)
                s_ve.inc(v.tensor_tensor(
                    out=vtmp[:, (tile % 4) * 512:(tile % 4 + 1) * 512],
                    in0=p.psA[:, (n3 % 2) * 512:(n3 % 2 + 1) * 512],
                    in1=bv_b[:, vb * 512:(vb + 1) * 512], op=ALU.add))
            block.vector(f_ve)

            if tile % 4 == 3:
                def f_vs(s, n3=n3, vb=vb, tile=tile, grp=grp):
                    s_ve.wait(s, s_ve.base + n3 + 1)
                    with s.register(f"avr{n3}") as r:
                        s.reg_alu(r, s.compute_val(p.core), 2, ALU.mod)
                        for half in range(2):
                            ctx = s.If_eq(r, 0) if half == 0 else s.If_ne(r, 0)
                            with ctx:
                                for tl in range(4):
                                    for hh in range(4):
                                        t0 = (tile - 3 + tl) * 128
                                        s.dma_start(
                                            out=p.sh_v[half, vb * 4 + hh, t0:t0 + 128, :],
                                            in_=vtmp[:, tl * 512 + hh * 128:
                                                     tl * 512 + (hh + 1) * 128]
                                        ).then_inc(s_vs.h, 16)
                    s_vs.n += 256
                block.sync(f_vs)
    es_v.close()

    def f_fl(s):
        s_ks.wait(s)
        s_vs.wait(s)
        with s.register("akvf") as r:
            s.reg_alu(r, s.compute_val(p.core), 2, ALU.mod)
            with s.If_eq(r, 0):
                s.dma_start(out=p.sh_flg[FLAG_KV, 0:1],
                            in_=p.fin[0:1, FLAG_KV:FLAG_KV + 1]).then_inc(s_fl.h, 16)
            with s.If_ne(r, 0):
                s.dma_start(out=p.sh_flg[FLAG_KV + 1, 0:1],
                            in_=p.fin[0:1, FLAG_KV + 1:FLAG_KV + 2]).then_inc(s_fl.h, 16)
        s_fl.n += 16
        s_fl.wait(s)
    block.sync(f_fl)

    es_k2 = ExitStack()
    wqk2 = es_k2.enter_context(nc.sbuf_tensor("at_wqk2", [128, 2 * 2048], BF16)).ap()
    for idx in range(16, 32):
        emit_qk_chunk(idx, idx - 16, wqk2)

    es_k2.close()
    es_1.close()

    es_2a = ExitStack()
    yT = es_2a.enter_context(nc.sbuf_tensor("at_yT", [128, AH * T], BF16)).ap()
    es_2b = ExitStack()
    sb2 = lambda n, s, d: es_2b.enter_context(nc.sbuf_tensor(n, s, d)).ap()
    kvk = sb2("at_kvk", [128, 2 * 4096], BF16)
    kvv = sb2("at_kvv", [128, 2 * 4096], BF16)
    pT = sb2("at_pT", [128, 2 * 512], BF16)
    rrow = sb2("at_rr", [1, 2 * 512], F32)
    den_b = sb2("at_den", [128, 2 * 512], F32)
    wo = sb2("at_wo", [128, 16 * 512], BF16)

    def f_poll(g):
        with g.register("apr") as r:
            g.reg_alu(r, g.compute_val(p.core), 2, ALU.mod)
            with g.If_eq(r, 0):
                poll_flag(p, g, FLAG_KV + 1)
            with g.If_ne(r, 0):
                poll_flag(p, g, FLAG_KV)
        s_kvf.inc(g.nop())
    block.gpsimd(f_poll)

    def f_wo0(g):
        s_qe.wait(g, s_qe.base + 128)
        for h in range(16):
            s_wo.dma(g.dma_start(out=wo[:, h * 512:(h + 1) * 512],
                                 in_=p.wout[h * 128:(h + 1) * 128, 0:512]))
    block.gpsimd(f_wo0)

    for h in range(AH):
        hp = h % 2

        def f_kvl(s, h=h, hp=hp):
            skv = s_kv[hp]
            if h == 0:
                s_kvf.wait(s)
                s_qe.wait(s, s_qe.base + 128)
            if h >= 2:
                s_pv.wait(s, s_pv.base + (h - 1) * 128)
            for hf in range(2):
                skv.dma(s.dma_start(
                    out=kvk[:, hp * 4096 + hf * 2048:hp * 4096 + (hf + 1) * 2048],
                    in_=p.sh_k[hf, h]))
                skv.dma(s.dma_start(
                    out=kvv[:, hp * 4096 + hf * 2048:hp * 4096 + (hf + 1) * 2048
                            ].rearrange("p (c j) -> p c j", j=128),
                    in_=p.sh_v[hf, h].rearrange("(c p) j -> p c j", p=128)))
        block.sync(f_kvl)

        for Qb in range(NQB):
            seg = h * NQB + Qb
            par = seg % 2
            ydst = yT[:, h * T + Qb * 512:h * T + (Qb + 1) * 512]

            def f_pe(t, h=h, hp=hp, Qb=Qb, seg=seg, par=par):
                s_kv[hp].wait(t, s_kv[hp].base + (h // 2 + 1) * 64)
                if seg >= 2:
                    s_ye.wait(t, s_ye.base + seg - 1)
                    s_rc.wait(t, s_rc.base + seg - 1)
                for c in range(NKC + 1):
                    n = seg * NKC + c
                    if c < NKC:
                        if n >= 2:
                            s_pe.wait(t, s_pe.base + n - 1)
                        s_sm.inc(t.matmul(
                            p.psA[:, (c % 2) * 512:(c % 2 + 1) * 512],
                            kvk[:, hp * 4096 + c * 128:hp * 4096 + (c + 1) * 128],
                            qT[:, h * T + Qb * 512:h * T + (Qb + 1) * 512],
                            start=True, stop=True))
                    if c >= 1:
                        cm = c - 1
                        nm = seg * NKC + cm
                        s_pe.wait(t, s_pe.base + nm + 1)
                        t.matmul(p.psB[:, par * 512:(par + 1) * 512],
                                 kvv[:, hp * 4096 + cm * 128:hp * 4096 + (cm + 1) * 128],
                                 pT[:, (cm % 2) * 512:(cm % 2 + 1) * 512],
                                 start=(cm == 0), stop=(cm == NKC - 1))
                        s_pv.inc(t.matmul(
                            p.psC[0:1, par * 512:(par + 1) * 512],
                            p.onescol[:, 0:1],
                            pT[:, (cm % 2) * 512:(cm % 2 + 1) * 512],
                            start=(cm == 0), stop=(cm == NKC - 1)))
            block.tensor(f_pe)

            def f_px(sc, seg=seg, par=par):
                for c in range(NKC):
                    n = seg * NKC + c
                    s_sm.wait(sc, s_sm.base + n + 1)
                    if n >= 2:
                        s_pv.wait(sc, s_pv.base + n - 1)
                    s_pe.inc(sc.activation(
                        out=pT[:, (c % 2) * 512:(c % 2 + 1) * 512],
                        in_=p.psA[:, (c % 2) * 512:(c % 2 + 1) * 512],
                        func=AF.Exp, scale=SCALE))
                s_pv.wait(sc, s_pv.base + (seg + 1) * NKC)
                if seg >= 2:
                    s_dsp.wait(sc, s_dsp.base + (seg - 1) * 16)
                p.chA.inc(sc.activation(out=rrow[0:1, par * 512:(par + 1) * 512],
                                        in_=p.psC[0:1, par * 512:(par + 1) * 512],
                                        func=AF.Ln))
                p.chA.wait(sc)
                s_rc.inc(sc.activation(out=rrow[0:1, par * 512:(par + 1) * 512],
                                       in_=rrow[0:1, par * 512:(par + 1) * 512],
                                       func=AF.Exp, scale=-1.0))
            block.scalar(f_px)

            def f_dsp(s, h=h, Qb=Qb, seg=seg, par=par):
                s_rc.wait(s, s_rc.base + seg + 1)
                if seg >= 2:
                    s_ye.wait(s, s_ye.base + seg - 1)
                s_dsp.dma(s.dma_start(out=p.dnm_sp[h:h + 1, Qb * 512:(Qb + 1) * 512],
                                      in_=rrow[0:1, par * 512:(par + 1) * 512]))
                s_dsp.wait(s)
                s_db[par].dma(s.dma_start(
                    out=den_b[:, par * 512:(par + 1) * 512],
                    in_=p.dnm_sp[h:h + 1, Qb * 512:(Qb + 1) * 512
                                 ].unsqueeze(1).broadcast_to([1, 128, 512])))
            block.sync(f_dsp)

            def f_ye(v, seg=seg, par=par, ydst=ydst):
                s_db[par].wait(v, s_db[par].base + (seg // 2 + 1) * 16)
                s_pv.wait(v, s_pv.base + (seg + 1) * NKC)
                s_ye.inc(v.tensor_tensor(out=ydst,
                                         in0=p.psB[:, par * 512:(par + 1) * 512],
                                         in1=den_b[:, par * 512:(par + 1) * 512],
                                         op=ALU.mult))
            block.vector(f_ye)

    xr2 = kvk.bitcast(F32)[:, 0:2 * 512]
    xo = kvv.bitcast(F32)[:, 0:2 * 512]

    for nbg in range(4):
        if nbg >= 1:
            def f_woq(g, nbg=nbg):
                s_om.wait(g, s_om.base + nbg * 16)
                for h in range(16):
                    s_wo.dma(g.dma_start(
                        out=wo[:, h * 512:(h + 1) * 512],
                        in_=p.wout[h * 128:(h + 1) * 128,
                                   nbg * 512:(nbg + 1) * 512]))
            block.gpsimd(f_woq)

        for tile in range(NTL):
            n4 = nbg * NTL + tile
            pst = (p.psA if tile % 2 == 0 else p.psB)[:, 0:512]

            def f_x2(s, nbg=nbg, tile=tile, n4=n4):
                if n4 < 2:
                    s_ye.wait(s, s_ye.base + 64)
                if n4 >= 2:
                    s_oe.wait(s, s_oe.base + n4 - 1)
                s_xr2[tile % 2].dma(s.dma_start(
                    out=xr2[:, (tile % 2) * 512:(tile % 2 + 1) * 512],
                    in_=p.x_res[tile * 128:(tile + 1) * 128,
                                nbg * 512:(nbg + 1) * 512]))
            block.sync(f_x2)

            def f_om(t, nbg=nbg, tile=tile, n4=n4, pst=pst):
                s_wo.wait(t, s_wo.base + (nbg + 1) * 256)
                if n4 == 0:
                    s_ye.wait(t, s_ye.base + 64)
                if n4 >= 2:
                    s_oe.wait(t, s_oe.base + n4 - 1)
                ins = None
                for h in range(16):
                    ins = t.matmul(
                        pst, yT[:, h * T + tile * 128:h * T + (tile + 1) * 128],
                        wo[:, h * 512:(h + 1) * 512],
                        start=(h == 0), stop=(h == 15))
                s_om.inc(ins)
            block.tensor(f_om)

            def f_oe(v, nbg=nbg, tile=tile, n4=n4, pst=pst):
                s_om.wait(v, s_om.base + n4 + 1)
                s_xr2[tile % 2].wait(v)
                if n4 >= 2:
                    s_ob.wait(v, s_ob.base + (n4 - 1) * 16)
                xov = xo[:, (tile % 2) * 512:(tile % 2 + 1) * 512]
                v.tensor_tensor(out=xov, in0=pst,
                                in1=xr2[:, (tile % 2) * 512:(tile % 2 + 1) * 512],
                                op=ALU.add)
                s_oe.inc(v.tensor_tensor(
                    out=xov, in0=xov,
                    in1=p.bout_b[:, nbg * 512:(nbg + 1) * 512], op=ALU.add))
            block.vector(f_oe)

            def f_ow(s, nbg=nbg, tile=tile, n4=n4):
                s_oe.wait(s, s_oe.base + n4 + 1)
                s_ob.dma(s.dma_start(
                    out=p.x_res[tile * 128:(tile + 1) * 128,
                                nbg * 512:(nbg + 1) * 512],
                    in_=xo[:, (tile % 2) * 512:(tile % 2 + 1) * 512]))
            block.sync(f_ow)

    def f_owf(s):
        s_ob.wait(s)
    block.sync(f_owf)

    es_2b.close()
    es_2a.close()
    es_q.close()
    barrier(p)
    p.phase_release()


def poll_flag(p, eng, flag_idx):
    nc = p.nc
    if not hasattr(p, "_pollsem"):
        p._pollsem = Sem(nc, "pollsem")
    ps = p._pollsem
    p._polln = getattr(p, "_polln", 0) + 1
    u = f"_{p._polln}"
    with eng.register("want" + u) as want, eng.register("fv" + u) as fv, \
         eng.register("lp" + u) as lp, eng.register("cnt" + u) as cnt:
        eng.reg_load(want, p.fin[0:1, flag_idx:flag_idx + 1])
        eng.reg_mov(cnt, 0)
        eng.reg_mov(lp, 1)
        with eng.While(lp):
            eng.dma_start(out=p.fpoll[0:1, 0:1],
                          in_=p.sh_flg[flag_idx:flag_idx + 1, 0:1]).then_inc(ps.h, 16)
            eng.reg_alu(cnt, cnt, 16, ALU.add)
            eng.wait_ge(ps.h, cnt)
            eng.reg_load(fv, p.fpoll[0:1, 0:1])
            eng.reg_alu(lp, fv, want, ALU.not_equal)
        if POLL_MODE >= 3:
            eng.sem_clear(ps.h)
        else:
            p._poll_noclear = getattr(p, "_poll_noclear", 0) + 1


def emit_output(p):
    nc, block = p.nc, p.block
    st = p.newsem("out_st")
    stg = p.newsem("out_stg")

    if OUT_SRC == "x_res":
        src_flat = p.x_res
        cast = False
    else:
        srcap = getattr(p, OUT_SRC)
        flat = srcap.flatten()
        n = flat.shape[0]
        take = min(n, T * D)
        src_flat = flat[0:take].rearrange("(r c) -> r c", c=2048)
        cast = (src_flat.dtype != F32)

    rows, cols = src_flat.shape

    def fg(g):
        if not cast:
            return
        with nc.sbuf_tensor("obufg", [128, 2 * D], F32) as ot:
            obuf = ot.ap()
            for tt in range((rows + 127) // 128):
                pr = tt % 2
                r1 = min(rows, (tt + 1) * 128)
                stg.dma(g.dma_start(out=obuf[0:r1 - tt * 128, pr * D:pr * D + cols],
                                    in_=src_flat[tt * 128:r1, :]))
                stg.wait(g)
                stg.dma(g.dma_start(out=p.y[tt * 128:r1, 0:cols],
                                    in_=obuf[0:r1 - tt * 128, pr * D:pr * D + cols]))
            stg.wait(g)
    if cast:
        block.gpsimd(fg)

    def f(s):
        if cast:
            return
        with nc.sbuf_tensor("obuf", [128, 2 * D], F32) as ot:
            obuf = ot.ap()
            for tt in range((rows + 127) // 128):
                pr = tt % 2
                r1 = min(rows, (tt + 1) * 128)
                st.dma(s.dma_start(out=obuf[0:r1 - tt * 128, pr * D:pr * D + cols],
                                   in_=src_flat[tt * 128:r1, :]))
                st.wait(s)
                st.dma(s.dma_start(out=p.y[tt * 128:r1, 0:cols],
                                   in_=obuf[0:r1 - tt * 128, pr * D:pr * D + cols]))
            st.wait(s)
            if hasattr(p, "s_ex"):
                with s.register("fexr") as r:
                    s.reg_alu(r, s.compute_val(p.core), 2, ALU.mod)
                    with s.If_eq(r, 0):
                        s.wait_ge(p.s_ex.h, p.s_ex.n)
    block.sync(f)

    def fg(g):
        g.drain()
    block.gpsimd(fg)
